# version 39
# baseline (speedup 1.0000x reference)
import numpy as np
import ml_dtypes
from contextlib import ExitStack

import concourse.bass as bass
import concourse.mybir as mybir
from concourse.alu_op_type import AluOpType as OP
from concourse.bass_utils import run_bass_kernel_spmd

F32 = mybir.dt.float32
BF16 = mybir.dt.bfloat16
I32 = mybir.dt.int32
AF = mybir.ActivationFunctionType
AX = mybir.AxisListType.X
BF = ml_dtypes.bfloat16

SEQ, D, H, DH, NW, WS, FF = 2048, 1024, 16, 64, 16, 128, 4096
NEG = -1.0e9
NCORE = 8
TRUNC = None


def host_consts():
    c = {}
    c["ident"] = np.eye(WS, dtype=BF)
    invf = (1.0 / 10000.0 ** (np.arange(0, DH, 2, dtype=np.float64) / DH))
    fr = invf[np.arange(DH) % 32]
    sign = np.where(np.arange(DH) < 32, -1.0, 1.0)

    def rot_tables(pos0, scale):
        ang = (pos0 + np.arange(WS))[:, None] * fr[None, :]
        cos = (scale * np.cos(ang)).astype(np.float32)
        sin = (scale * np.sin(ang) * sign[None, :]).astype(np.float32)
        return cos, sin
    c["cosq"], c["sinq"] = rot_tables(2 * WS, 8.0)
    c["cosk"], c["sink"] = rot_tables(WS, 1.0)
    d_of_p = np.arange(128) % DH
    delta = WS * fr[d_of_p]
    c["cD"] = np.cos(delta).astype(np.float32)[:, None]
    sD = (np.sin(delta) * sign[d_of_p])[:, None]
    c["sDp"] = sD.astype(np.float32)
    c["sDm"] = (-sD).astype(np.float32)
    sig_p = np.where((d_of_p % DH) < 32, np.arange(128) + 32,
                     np.arange(128) - 32)
    R2 = np.zeros((128, 128), np.float64)
    R2[np.arange(128), np.arange(128)] = np.cos(delta)
    R2[np.arange(128), sig_p] = sD[:, 0]
    R0 = np.zeros((128, 128), np.float64)
    R0[np.arange(128), np.arange(128)] = np.cos(delta)
    R0[np.arange(128), sig_p] = -sD[:, 0]
    for nm, R in (("R0T", R0.T), ("R2T", R2.T)):
        hi = np.ascontiguousarray(R).astype(BF)
        lo = (R - hi.astype(np.float64)).astype(BF)
        c[nm + "h"] = hi
        c[nm + "l"] = np.ascontiguousarray(lo)
    kc = np.arange(WS)[:, None]
    qq = np.arange(WS)[None, :]
    tri0 = (kc >= qq).astype(np.float64)
    tri2 = (kc <= qq).astype(np.float64)
    c["maskcat"] = np.concatenate([tri0, tri2], axis=1).astype(BF)
    vones = np.zeros((128, 4, H, 65), np.float64)
    vones[:, :, :, 64] = 1.0
    c["vones"] = vones.reshape(128, 4 * H * 65).astype(BF)
    c["onesr"] = np.ones((1, 128), dtype=np.float32)
    c["onesc"] = np.ones((128, 1), dtype=np.float32)
    c["cvals"] = np.broadcast_to(
        np.array([1e-5, 1e-24, 1e-20, 1e-30], np.float32), (128, 4)).copy()
    return c


SIG = np.r_[np.arange(32, 64), np.arange(0, 32)]


def host_inputs(inp):
    d = {}
    d["wqkv"] = np.ascontiguousarray(inp["L0_attn_w_qkv"], np.float32)
    d["wout"] = np.ascontiguousarray(inp["L0_attn_w_out"], np.float32)
    qs = np.asarray(inp["L0_attn_q_scale"], np.float32)
    ks = np.asarray(inp["L0_attn_k_scale"], np.float32)
    d["qs_t"] = np.ascontiguousarray(np.broadcast_to(qs, (WS, DH)))
    d["qs_s"] = np.ascontiguousarray(np.broadcast_to(qs[SIG], (WS, DH)))
    d["ks_t"] = np.ascontiguousarray(np.broadcast_to(ks, (WS, DH)))
    d["ks_s"] = np.ascontiguousarray(np.broadcast_to(ks[SIG], (WS, DH)))
    d["w1t"] = np.ascontiguousarray(np.asarray(inp["L0_ffn_W1"], np.float32).T)
    d["w2t"] = np.ascontiguousarray(np.asarray(inp["L0_ffn_W2"], np.float32).T)
    d["b1t"] = np.ascontiguousarray(
        np.asarray(inp["L0_ffn_b1"], np.float32).reshape(32, 128).T)
    d["b2b"] = np.ascontiguousarray(
        np.broadcast_to(np.asarray(inp["L0_ffn_b2"], np.float32), (128, D)))
    return d


def build_nc(taps=()):
    nc = bass.Bass("TRN2", target_bir_lowering=False)
    EM = {}

    def din(name, shape, dt=F32):
        return nc.dram_tensor(name, shape, dt, kind="ExternalInput").ap()

    x_d = din("x", [SEQ, D])
    wqkv_d = din("wqkv", [D, 3 * D])
    wout_d = din("wout", [D, D])
    qs_t_d = din("qs_t", [WS, DH]); qs_s_d = din("qs_s", [WS, DH])
    ks_t_d = din("ks_t", [WS, DH]); ks_s_d = din("ks_s", [WS, DH])
    w1t_d = din("w1t", [D, FF])
    w2t_d = din("w2t", [FF, D])
    b1t_d = din("b1t", [128, 32])
    b2b_d = din("b2b", [128, D])
    ident_d = din("ident", [WS, WS], BF16)
    cosq_d = din("cosq", [WS, DH]); sinq_d = din("sinq", [WS, DH])
    cosk_d = din("cosk", [WS, DH]); sink_d = din("sink", [WS, DH])
    cD_d = din("cD", [128, 1]); sDp_d = din("sDp", [128, 1])
    sDm_d = din("sDm", [128, 1])
    R0Th_d = din("R0Th", [128, 128], BF16)
    R0Tl_d = din("R0Tl", [128, 128], BF16)
    R2Th_d = din("R2Th", [128, 128], BF16)
    R2Tl_d = din("R2Tl", [128, 128], BF16)
    maskcat_d = din("maskcat", [128, 2 * WS], BF16)
    vones_d = din("vones", [128, 4 * H * 65], BF16)
    onesr_d = din("onesr", [1, 128]); onesc_d = din("onesc", [128, 1])
    cvals_d = din("cvals", [128, 4])

    out_d = nc.dram_tensor("out", [SEQ, D], F32, kind="ExternalOutput").ap()
    s_scr = nc.dram_tensor("s_scr", [SEQ, D], BF16, kind="Internal").ap()

    tap_d = {}
    for tname, tshape, tdt in taps:
        tap_d[tname] = nc.dram_tensor("tap_" + tname, tshape, tdt,
                                      kind="ExternalOutput").ap()

    x_t4 = x_d.rearrange("(w p) n -> w p n", p=WS)
    s_t4 = s_scr.rearrange("(w p) n -> w p n", p=WS)
    out_t4 = out_d.rearrange("(w p) n -> w p n", p=WS)
    wqkv_t = wqkv_d.rearrange("(k p) n -> k p n", p=WS)
    wout_t = wout_d.rearrange("(k p) n -> k p n", p=WS)
    w1t_t = w1t_d.rearrange("(k p) (c n) -> k c p n", p=WS, n=1024)
    w2t_t = w2t_d.rearrange("(c k p) n -> c k p n", p=WS, k=8)

    class S:
        def __init__(self, name):
            self.h = stack.enter_context(nc.semaphore(name))
            self.c = 0

    stack = ExitStack()
    SD = None

    class _DummyInst:
        def then_inc(self, *a, **k):
            return self

    class _Proxy:
        def __getattr__(self, name):
            def f(*a, **k):
                return _DummyInst()
            return f

    BODIES = []
    PASS = {"n": 1}

    def on(eng, body):
        BODIES.append((eng, body))
        body(_Proxy())

    def inc2(inst, sem, tag=None, n=1):
        if PASS["n"] == 2:
            inst.then_inc(sem.h, n)
        sem.c += n
        if tag is not None:
            if PASS["n"] == 1:
                assert tag not in EM, tag
                EM[tag] = sem.c
            else:
                assert EM[tag] == sem.c, (tag, EM[tag], sem.c)

    def incm2(inst, sem, tags, n=None):
        n = len(tags) if n is None else n
        if PASS["n"] == 2:
            inst.then_inc(sem.h, n)
        sem.c += n
        for tag in tags:
            if PASS["n"] == 1:
                assert tag not in EM, tag
                EM[tag] = sem.c
            else:
                assert EM[tag] == sem.c, (tag, EM[tag], sem.c)

    def incm(e, sem, tags, n=None):
        n = len(tags) if n is None else n
        if PASS["n"] == 2:
            e.drain().then_inc(sem.h, n)
        sem.c += n
        for tag in tags:
            if PASS["n"] == 1:
                assert tag not in EM, tag
                EM[tag] = sem.c
            else:
                assert EM[tag] == sem.c, (tag, EM[tag], sem.c)

    def inc(e, sem, tag=None, n=1):
        if PASS["n"] == 2:
            e.drain().then_inc(sem.h, n)
        sem.c += n
        if tag is not None:
            if PASS["n"] == 1:
                assert tag not in EM, tag
                EM[tag] = sem.c
            else:
                assert EM[tag] == sem.c, (tag, EM[tag], sem.c)

    SYNC_RING = {"sd": 0, "so": 0}

    def sync_dma(e, out_ap, in_ap, sem, tag=None):
        if PASS["n"] == 2:
            if SYNC_RING["sd"] + SYNC_RING["so"] > 0:
                e.wait_ge(SD.h, SYNC_RING["sd"] * 16)
                if SYNC_RING["so"]:
                    e.wait_ge(SO.h, SYNC_RING["so"] * 16)
            e.dma_start(out_ap, in_ap).then_inc(sem.h, 16)
        sem.c += 16
        if sem is SD:
            SYNC_RING["sd"] = sem.c // 16
        else:
            SYNC_RING["so"] = sem.c // 16
        if tag is not None:
            if PASS["n"] == 1:
                assert tag not in EM, tag
                EM[tag] = sem.c
            else:
                assert EM[tag] == sem.c, (tag, EM[tag], sem.c)

    POOL_RING = {"n": 0}

    def pool_dma(e, out_ap, in_ap, tag=None):
        if PASS["n"] == 2:
            e.wait_ge(SGD.h, SGD.c)
            e.dma_start(out_ap, in_ap).then_inc(SGD.h, 16)
        SGD.c += 16
        if tag is not None:
            if PASS["n"] == 1:
                assert tag not in EM, tag
                EM[tag] = SGD.c
            else:
                assert EM[tag] == SGD.c, (tag, EM[tag], SGD.c)

    def gap(e):
        if PASS["n"] == 2:
            e.drain()

    def wt(e, sem, tag):
        if PASS["n"] == 1:
            assert tag in EM, f"wait on not-yet-produced mark {tag}"
        else:
            e.wait_ge(sem.h, EM[tag])

    with stack:
        SD = S("SD")
        SC = S("SC")
        SGD = S("SGD")
        SP = S("SP")
        SV = S("SV")
        SA = S("SA")
        SG = S("SG")
        SO = S("SO")
        MASK_ENG, MASK_SEM = "gpsimd", SG
        EMIT_MASKS = True

        P = {}

        def sb(name, shape, dt=F32, pool=stack):
            P[name] = pool.enter_context(nc.sbuf_tensor("sb_" + name, shape,
                                                        dt))
            return P[name]

        xnT = sb("xnT", [128, 8, SEQ], BF16)
        ident = sb("ident", [WS, WS], BF16)
        onesr = sb("onesr", [1, 128]); onesc = sb("onesc", [128, 1])
        partials1 = sb("partials1", [128, 32])
        partials2 = sb("partials2", [128, 32])
        rs1col = sb("rs1col", [128, 1]); rs2col = sb("rs2col", [128, 1])
        smallA = sb("smallA", [1, 8])
        smallB = sb("smallB", [128, 1])
        cvals = sb("cvals", [128, 4])

        p1 = ExitStack()

        def sb1(name, shape, dt=F32):
            return sb(name, shape, dt, pool=p1)

        wqkv = sb1("wqkv", [128, 8, 3 * D], BF16)
        woutW = sb1("woutW", [128, 8, D], BF16)
        x_t = sb1("x_t", [128, 2, D])
        xn_t = sb1("xn_t", [128, 2, D], BF16)
        sq1 = sb1("sq1", [128, D]); sq2 = sb1("sq2", [128, D])
        q_hat = sb1("q_hat", [128, D])
        k_hat = sb1("k_hat", [128, D])
        rtmp = sb1("rtmp", [128, D])
        krot1 = sb1("krot1", [128, D], BF16)
        bq = sb1("bq", [128, 3, D], BF16)
        bqT = sb1("bqT", [128, 2, 8, WS], BF16)
        bkT = sb1("bkT", [128, 3, 8, 3 * WS], BF16)
        R0Ts = sb1("R0Ts", [128, 2, WS], BF16)
        R2Ts = sb1("R2Ts", [128, 2, WS], BF16)
        v_ring = sb1("v_ring", [128, 4, H * 65], BF16)
        pTall = sb1("pTall", [128, 4, 3 * WS], BF16)
        aoALL = sb1("aoALL", [128, D], BF16)
        aoT = sb1("aoT", [128, 8, WS], BF16)
        s_tile = sb1("s_tile", [128, 2, D], BF16)
        stats = sb1("stats", [128, 8])
        n2 = sb1("n2", [128, H]); rn = sb1("rn", [128, H])
        den = sb1("den", [128, 2, H]); rden = sb1("rden", [128, 2, H])
        Cq = sb1("Cq", [WS, DH]); Sq = sb1("Sq", [WS, DH])
        Ck = sb1("Ck", [WS, DH]); Sk = sb1("Sk", [WS, DH])
        qs_sb = sb1("qs_sb", [WS, DH]); qss_sb = sb1("qss_sb", [WS, DH])
        ks_sb = sb1("ks_sb", [WS, DH]); kss_sb = sb1("kss_sb", [WS, DH])
        tbl = sb1("tbl", [WS, 4, DH])
        maskcat = sb1("maskcat", [128, 2 * WS], BF16)
        negBcol = sb1("negBcol", [128, 1])
        wst = sb1("wst", [128, 4, 1024])

        pA = p1.enter_context(nc.psum_tensor("pA", [128, D], F32))
        pB = p1.enter_context(nc.psum_tensor("pB", [128, 1536], F32))
        pC = p1.enter_context(nc.psum_tensor("pC", [128, 1024], F32))
        pE = p1.enter_context(nc.psum_tensor("pE", [128, D], BF16))

        def av_dst(h):
            if h < 7:
                off = 65 * h
            elif h < 14:
                off = 512 + 65 * (h - 7)
            else:
                off = 1024 + 65 * (h - 14)
            return pB[:, off:off + 65]

        AVG = [(0, 7, 0), (512, 7, 7), (1024, 2, 14)]
        CPERM = {0: 0, 1: 2 * WS, 2: WS}

        def setup_sync(e):
            e.dma_start(ident[:, :], ident_d).then_inc(SC.h, 16)
            e.dma_start(onesr[:, :], onesr_d).then_inc(SC.h, 16)
            e.dma_start(onesc[:, :], onesc_d).then_inc(SC.h, 16)
            e.dma_start(cvals[:, :], cvals_d).then_inc(SC.h, 16)
            e.dma_start(tbl[:, 0, :], cosq_d).then_inc(SC.h, 16)
            e.dma_start(tbl[:, 1, :], sinq_d).then_inc(SC.h, 16)
            e.dma_start(tbl[:, 2, :], cosk_d).then_inc(SC.h, 16)
            e.dma_start(tbl[:, 3, :], sink_d).then_inc(SC.h, 16)
            e.dma_start(qs_sb[:, :], qs_t_d).then_inc(SC.h, 16)
            e.dma_start(qss_sb[:, :], qs_s_d).then_inc(SC.h, 16)
            e.dma_start(ks_sb[:, :], ks_t_d).then_inc(SC.h, 16)
            e.dma_start(kss_sb[:, :], ks_s_d).then_inc(SC.h, 16)
            e.dma_start(R0Ts[:, 0, :], R0Th_d).then_inc(SC.h, 16)
            e.dma_start(R0Ts[:, 1, :], R0Tl_d).then_inc(SC.h, 16)
            e.dma_start(R2Ts[:, 0, :], R2Th_d).then_inc(SC.h, 16)
            e.dma_start(R2Ts[:, 1, :], R2Tl_d).then_inc(SC.h, 16)
            e.dma_start(maskcat[:, :], maskcat_d).then_inc(SC.h, 16)
            e.dma_start(v_ring[:, :, :],
                        vones_d.rearrange("p (a b) -> p a b",
                                          b=H * 65)).then_inc(SC.h, 16)
            SC.c += 16 * 18
            EM["setup_sync"] = SC.c

        on("sync", setup_sync)

        def setup_gp(e):
            for lo, hi, tag in ((D, 2 * D, "setup_k"),
                                (2 * D, 3 * D, "setup_v"),
                                (0, D, "setup_q")):
                for k in range(8):
                    e.dma_start(wqkv[:, k, lo:hi],
                                wqkv_t[k][:, lo:hi]).then_inc(SGD.h, 16)
                SGD.c += 16 * 8
                EM[tag] = SGD.c
                e.wait_ge(SGD.h, SGD.c)
            for k in range(8):
                e.dma_start(woutW[:, k, :], wout_t[k]).then_inc(SGD.h, 16)
            SGD.c += 16 * 8
            EM["setup_gp"] = SGD.c

        on("gpsimd", setup_gp)

        def setup_dve(e):
            wt(e, SC, "setup_sync")
            e.tensor_tensor(Cq[:, :], tbl[:, 0, :], qs_sb[:, :], OP.mult)
            e.tensor_tensor(Sq[:, :], tbl[:, 1, :], qss_sb[:, :], OP.mult)
            e.tensor_tensor(Ck[:, :], tbl[:, 2, :], ks_sb[:, :], OP.mult)
            e.tensor_tensor(Sk[:, :], tbl[:, 3, :], kss_sb[:, :], OP.mult)
            e.tensor_reduce(smallA[0:1, 0:1], ks_sb[0:1, :], AX, OP.max,
                            apply_absolute_value=True)
            e.tensor_reduce(smallA[0:1, 4:5], qs_sb[0:1, :], AX, OP.max,
                            apply_absolute_value=True)
            gap(e)
            e.tensor_tensor(smallA[0:1, 5:6], smallA[0:1, 0:1],
                            smallA[0:1, 4:5], OP.mult)
            gap(e)
            i2 = e.tensor_scalar(smallA[0:1, 1:2], smallA[0:1, 5:6],
                                 -8.0, None, OP.mult)
            inc(e, SV, "setup_dve")

        on("vector", setup_dve)

        def setup_pe(e):
            wt(e, SV, "setup_dve")
            i = e.matmul(pC[:, 0:1], onesr[:, :], smallA[0:1, 1:2],
                         start=True, stop=True)
            inc(e, SP, "setup_pe_mk2")

        on("tensor", setup_pe)

        def setup_dve2(e):
            wt(e, SP, "setup_pe_mk2")
            i = e.tensor_copy(negBcol[:, :], pC[:, 0:1])
            inc(e, SV, "setup_done_dve")

        on("vector", setup_dve2)

        def tap(name, ap_fn, wait=None):
            if name not in tap_d:
                return

            def body(e):
                if wait is not None:
                    sem, tag = wait
                    wt(e, sem, tag)
                sync_dma(e, tap_d[name], ap_fn(), SO, "tap_" + name)

            on("sync", body)

        vslot = {w: (w % 4) for w in range(-1, NW + 1)}
        vslot[-1] = 4

        W1TILES = [("w1", k, cc) for k in range(8) for cc in range(4)]
        W2TILES = [("w2", c, k) for c in range(4) for k in range(8)]
        WTILES = W1TILES + W2TILES

        for w in range((NW + 2) if TRUNC is None else TRUNC):
            qw = w - 2
            hasA = w <= NW - 1
            hasB = qw >= 0

            def sync_a(e, w=w):
                for v in ([0, 1] if w == 0 else [w + 1]):
                    if v > NW - 1:
                        continue
                    if v >= 2:
                        wt(e, SV, f"ln_{v-2}")
                    sync_dma(e, x_t[:, v % 2, :], x_t4[v], SD, f"x_{v}")
            on("sync", sync_a)

            def emit_ln_var(v):
                def dve_ln(e, v=v):
                    wt(e, SD, f"x_{v}")
                    xt = x_t[:, v % 2, :]
                    e.tensor_reduce(stats[:, 0:1], xt, AX, OP.add)
                    gap(e)
                    e.tensor_scalar(stats[:, 1:2], stats[:, 0:1],
                                    -1.0 / D, None, OP.mult)
                    gap(e)
                    i = e.scalar_tensor_tensor(sq1[:, :], xt, stats[:, 1:2],
                                               xt, OP.add, OP.mult,
                                               accum_out=stats[:, 2:3])
                    inc(e, SV, f"lnvar_{v}")

                on("vector", dve_ln)

            def emit_ln_rest(v):
                def act_ln0(e, v=v):
                    wt(e, SV, f"lnvar_{v}")
                    e.activation(stats[:, 3:4], stats[:, 2:3], AF.Ln,
                                 bias=cvals[:, 0:1], scale=1.0 / D)
                    gap(e)
                    i = e.activation(stats[:, 4:5], stats[:, 3:4],
                                     AF.Exp, scale=-0.5)
                    inc(e, SA, f"lnsd_{v}")

                on("scalar", act_ln0)

                def dve_ln20(e, v=v):
                    wt(e, SA, f"lnsd_{v}")
                    if v >= 2:
                        wt(e, SP, f"pe_xnT_{v-2}")
                    i = e.tensor_scalar(xn_t[:, v % 2, :],
                                        x_t[:, v % 2, :],
                                        stats[:, 1:2], stats[:, 4:5],
                                        OP.add, OP.mult)
                    inc2(i, SV, f"ln_{v}")

                on("vector", dve_ln20)

            if w == 0:
                emit_ln_var(0)
                emit_ln_rest(0)
            if w + 1 <= NW - 1:
                emit_ln_var(w + 1)
                if w < 2:
                    emit_ln_rest(w + 1)



            def emit_bqT_early(w=w):
                nqw = w - 1

                def pe_bqT(e, nqw=nqw, w=w):
                    wt(e, SV, f"bq_{nqw}")
                    wt(e, SA, f"kT1_{nqw}")
                    for c in range(8):
                        i = e.transpose(pE[:, c * WS:(c + 1) * WS],
                                        bq[:, nqw % 3, c * WS:(c + 1) * WS],
                                        ident[:, :])
                    inc2(i, SP, f"pe_bqT_{nqw}")

                on("tensor", pe_bqT)

                def dve_bqT(e, nqw=nqw):
                    wt(e, SP, f"pe_bqT_{nqw}")
                    i = e.tensor_copy(bqT[:, nqw % 2, :, :],
                                      pE[:, :].rearrange(
                                          "p (c n) -> p c n", n=WS))
                    inc2(i, SV, f"bqTev_{nqw}")

                on("vector", dve_bqT)

            if w == 1:
                emit_bqT_early()
            if hasB:
                c_list = ([1, 2] if qw == 0 else
                          ([0, 1] if qw == NW - 1 else [0, 1, 2]))

                def emit_sim(h, qw=qw, c_list=c_list):
                    def pe_sim(e, h=h):
                        if h == 0:
                            if qw == 0:
                                wt(e, SV, "setup_done_dve")
                            wt(e, SA, f"kT1_{qw}")
                            if qw <= NW - 2:
                                wt(e, SV, f"cr2_{qw}")
                            else:
                                wt(e, SV, f"bqTev_{qw}")
                        if h >= 2:
                            wt(e, SA, f"exp_{qw}_{h-2}")
                        elif qw >= 1:
                            wt(e, SA, f"exp_{qw-1}_{h+14}")
                        ph = (h % 2) * DH
                        ch = h // 2
                        b = (h % 2) * 512
                        for c in c_list:
                            i = e.matmul(
                                pC[:, b + CPERM[c]:b + CPERM[c] + WS],
                                bkT[ph:ph + DH, qw % 3, ch,
                                    c * WS:(c + 1) * WS],
                                bqT[ph:ph + DH, qw % 2, ch, :],
                                start=True, stop=True)
                        LASTI[0] = i

                    on("tensor", pe_sim)

                def emit_exp(h, qw=qw):
                    def act_exp(e, h=h):
                        wt(e, SP, f"pe_sim_{qw}_{h}")
                        if qw == 0 and h == 0:
                            wt(e, SV, "setup_done_dve")
                        if h >= 4:
                            wt(e, SP, f"pe_av_{qw}_{h-4}")
                        elif qw >= 1:
                            wt(e, SP, f"pe_av_{qw-1}_{h+12}")
                        b = (h % 2) * 512
                        s = h % 4
                        if qw == 0:
                            i = e.activation(pTall[:, s, WS:3 * WS],
                                             pC[:, b + WS:b + 3 * WS],
                                             AF.Exp,
                                             bias=negBcol[:, 0:1], scale=1.0)
                        elif qw == NW - 1:
                            e.activation(pTall[:, s, 0:WS],
                                         pC[:, b:b + WS], AF.Exp,
                                         bias=negBcol[:, 0:1], scale=1.0)
                            i = e.activation(pTall[:, s, 2 * WS:3 * WS],
                                             pC[:, b + 2 * WS:b + 3 * WS],
                                             AF.Exp,
                                             bias=negBcol[:, 0:1], scale=1.0)
                        else:
                            i = e.activation(pTall[:, s, :],
                                             pC[:, b:b + 3 * WS], AF.Exp,
                                             bias=negBcol[:, 0:1], scale=1.0)
                        inc2(i, SA, f"exp_{qw}_{h}")

                    on("scalar", act_exp)

                def emit_mask(h, qw=qw):
                    def gp_mask(e, h=h):
                        wt(e, SA, f"exp_{qw}_{h}")
                        s = h % 4
                        if qw == 0:
                            lo, hi = WS, 2 * WS
                        elif qw == NW - 1:
                            lo, hi = 0, WS
                        else:
                            lo, hi = 0, 2 * WS
                        i = e.tensor_tensor(pTall[:, s, lo:hi],
                                            pTall[:, s, lo:hi],
                                            maskcat[:, lo:hi], OP.mult)
                        inc2(i, MASK_SEM, f"mask_{qw}_{h}")

                    on(MASK_ENG, gp_mask)

                def emit_av(h, qw=qw, c_list=c_list):
                    def pe_av(e, h=h):
                        if EMIT_MASKS:
                            wt(e, MASK_SEM, f"mask_{qw}_{h}")
                        else:
                            wt(e, SA, f"exp_{qw}_{h}")
                        if h == 0:
                            if qw <= NW - 2:
                                wt(e, SV, f"cr2_{qw}")
                            else:
                                wt(e, SV, f"ao_{qw-1}")
                            wt(e, SA, f"vev_{min(qw + 1, NW - 1)}")
                        s = h % 4
                        first, last = c_list[0], c_list[-1]
                        for c in c_list:
                            slot = (qw - 1 + c) % 4
                            i = e.matmul(
                                av_dst(h),
                                pTall[:, s, CPERM[c]:CPERM[c] + WS],
                                v_ring[:, slot, h * 65:(h + 1) * 65],
                                start=(c == first), stop=(c == last))
                        LASTI[0] = i

                    on("tensor", pe_av)

                LASTI = [None]
                for h in range(H + 3):
                    tags = []
                    if h < H:
                        emit_sim(h)
                        tags.append(f"pe_sim_{qw}_{h}")
                    if h >= 3:
                        emit_av(h - 3)
                        tags.append(f"pe_av_{qw}_{h-3}")

                    def pe_hmark(e, tags=tuple(tags)):
                        incm2(LASTI[0], SP, list(tags))

                    on("tensor", pe_hmark)
                    if h < H:
                        emit_exp(h)
                        if EMIT_MASKS:
                            emit_mask(h)

                if w <= NW:
                    emit_bqT_early()

                def dve_dengath(e, qw=qw):
                    wt(e, SP, f"pe_av_{qw}_15")
                    if qw >= 2:
                        wt(e, SV, f"rden_{qw-2}")
                    for off, nh, h0 in AVG:
                        i = e.tensor_scalar(
                            den[:, qw % 2, h0:h0 + nh].unsqueeze(2),
                            pB[:, off:off + nh * 65].rearrange(
                                "p (h c) -> p h c", c=65)[:, :, 64:65],
                            1e-30, None, OP.add)
                    inc2(i, SV, f"dengath_{qw}")

                on("vector", dve_dengath)

                def dve_rden(e, qw=qw):
                    if qw >= 2:
                        wt(e, SV, f"ao_{qw-2}")
                    gap(e)
                    i = e.reciprocal(rden[:, qw % 2, :], den[:, qw % 2, :])
                    inc(e, SV, f"rden_{qw}")

                on("vector", dve_rden)

                def dve_ao(e, qw=qw):
                    wt(e, SV, f"rden_{qw}")
                    if qw >= 1:
                        wt(e, SP, f"pe_aoT_{qw-1}")
                    for off, nh, h0 in AVG:
                        i = e.tensor_tensor(
                            aoALL[:, h0 * DH:(h0 + nh) * DH].rearrange(
                                "p (h d) -> p h d", d=DH),
                            pB[:, off:off + nh * 65].rearrange(
                                "p (h c) -> p h c", c=65)[:, :, 0:64],
                            rden[:, qw % 2, h0:h0 + nh].unsqueeze(2)
                                .broadcast_to((128, nh, DH)),
                            OP.mult)
                    inc2(i, SV, f"ao_{qw}")

                on("vector", dve_ao)

                if w + 1 <= NW - 1:
                    emit_ln_rest(w + 1)

                def pe_aoT(e, qw=qw):
                    wt(e, SV, f"ao_{qw}")
                    wt(e, SV, f"bqTev_{min(qw + 1, NW - 1)}")
                    for c in range(8):
                        i = e.transpose(pE[:, c * WS:(c + 1) * WS],
                                        aoALL[:, c * WS:(c + 1) * WS],
                                        ident[:, :])
                    inc2(i, SP, f"pe_aoT_{qw}")

                on("tensor", pe_aoT)

                def act_aoT(e, qw=qw):
                    wt(e, SP, f"pe_aoT_{qw}")
                    if qw >= 1:
                        wt(e, SP, f"pe_wout_{qw-1}")
                    i = e.copy(aoT[:, :, :],
                               pE[:, :].rearrange("p (c n) -> p c n", n=WS))
                    inc2(i, SA, f"aoTev_{qw}")

                on("scalar", act_aoT)

                def pe_wout(e, qw=qw, w=w):
                    wt(e, SA, f"aoTev_{qw}")
                    if qw <= 1:
                        wt(e, SGD, "setup_gp")
                    wt(e, SV, f"qhat_{min(w - 1, NW - 1)}")
                    for k in range(8):
                        for hf in range(2):
                            i = e.matmul(pA[:, hf * 512:(hf + 1) * 512],
                                         aoT[:, k, :],
                                         woutW[:, k, hf * 512:(hf + 1) * 512],
                                         start=(k == 0), stop=(k == 7))
                    inc2(i, SP, f"pe_wout_{qw}")

                on("tensor", pe_wout)

                def act_s(e, qw=qw):
                    wt(e, SP, f"pe_wout_{qw}")
                    if qw >= 2:
                        wt(e, SD, f"sst_{qw-2}")
                    i = e.copy(s_tile[:, qw % 2, :], pA[:, :])
                    inc2(i, SA, f"sev_{qw}")

                on("scalar", act_s)

                def sync_s(e, qw=qw):
                    wt(e, SA, f"sev_{qw}")
                    sync_dma(e, s_t4[qw], s_tile[:, qw % 2, :], SD,
                             f"sst_{qw}")

                on("sync", sync_s)

            def emit_wtile(t, w=w):
                kind, a, b = WTILES[t]

                def gp_wst(e, t=t, kind=kind, a=a, b=b):
                    if t >= 4:
                        sem = SV if ((t - 4) % 2 == 0) else SA
                        wt(e, sem, f"absred_{t-4}")
                    src = w1t_t[a, b] if kind == "w1" else w2t_t[a, b]
                    pool_dma(e, wst[:, t % 4, :], src, f"wst_{t}")

                on("gpsimd", gp_wst)

                part = partials1 if kind == "w1" else partials2
                col = (t - (0 if kind == "w1" else 32))
                if t % 2 == 0:
                    def dve_abs(e, t=t, part=part, col=col):
                        wt(e, SGD, f"wst_{t}")
                        i = e.tensor_reduce(part[:, col:col + 1],
                                            wst[:, t % 4, :], AX, OP.add,
                                            apply_absolute_value=True)
                        inc2(i, SV, f"absred_{t}")

                    on("vector", dve_abs)
                else:
                    def act_abs(e, t=t, part=part, col=col):
                        wt(e, SGD, f"wst_{t}")
                        i = e.activation(sq2[:, :], wst[:, t % 4, :],
                                         AF.Abs,
                                         accum_out=part[:, col:col + 1])
                        inc2(i, SA, f"absred_{t}")

                    on("scalar", act_abs)

            if hasA:
                def pe_xnT(e, w=w):
                    wt(e, SV, f"ln_{w}")
                    if w == 0:
                        wt(e, SC, "setup_sync")
                    elif w == 1:
                        wt(e, SV, "bqTev_0")
                    else:
                        wt(e, SA, f"aoTev_{w-2}")
                    for c in range(8):
                        i = e.transpose(pE[:, c * WS:(c + 1) * WS],
                                        xn_t[:, w % 2, c * WS:(c + 1) * WS],
                                        ident[:, :])
                    inc2(i, SP, f"pe_xnT_{w}")

                on("tensor", pe_xnT)

                def dve_xnT(e, w=w):
                    wt(e, SP, f"pe_xnT_{w}")
                    i = e.tensor_copy(xnT[:, :, w * WS:(w + 1) * WS],
                                      pE[:, :].rearrange(
                                          "p (c n) -> p c n", n=WS))
                    inc2(i, SV, f"xnTev_{w}")

                on("vector", dve_xnT)

                def pe_k(e, w=w):
                    wt(e, SV, f"xnTev_{w}")
                    if w <= 1:
                        wt(e, SGD, "setup_k")
                    if w == 1:
                        wt(e, SV, "khat_0")
                    elif w >= 2:
                        wt(e, SV, f"ao_{w-2}")
                    for k in range(8):
                        for hf in range(2):
                            i = e.matmul(
                                pB[:, hf * 512:(hf + 1) * 512],
                                xnT[:, k, w * WS:(w + 1) * WS],
                                wqkv[:, k, D + hf * 512:D + (hf + 1) * 512],
                                start=(k == 0), stop=(k == 7))
                    inc2(i, SP, f"pe_k_{w}")

                on("tensor", pe_k)

                def act_sqk(e, w=w):
                    wt(e, SP, f"pe_k_{w}")
                    if w >= 1:
                        wt(e, SV, f"redk_{w-1}")
                    i = e.activation(sq2[:, :], pB[:, 0:D], AF.Square)
                    inc2(i, SA, f"sqk_{w}")

                on("scalar", act_sqk)

                def dve_redk(e, w=w):
                    wt(e, SA, f"sqk_{w}")
                    i = e.tensor_reduce(
                        n2[:, :], sq2[:, :].rearrange("p (h d) -> p h d", d=DH),
                        AX, OP.add)
                    inc(e, SV, f"redk_{w}")

                on("vector", dve_redk)

                def act_sqrtk(e, w=w):
                    wt(e, SV, f"redk_{w}")
                    e.activation(n2[:, :], n2[:, :], AF.Ln,
                                 bias=cvals[:, 1:2])
                    gap(e)
                    i = e.activation(rn[:, :], n2[:, :], AF.Exp, scale=-0.5)
                    inc(e, SA, f"sqrtk_{w}")

                on("scalar", act_sqrtk)

                def dve_khat(e, w=w):
                    wt(e, SA, f"sqrtk_{w}")
                    i = e.tensor_tensor(
                        k_hat[:, :].rearrange("p (h d) -> p h d", d=DH),
                        pB[:, 0:D].rearrange("p (h d) -> p h d", d=DH),
                        rn[:, :].unsqueeze(2).broadcast_to((128, H, DH)),
                        OP.mult)
                    inc2(i, SV, f"khat_{w}")

                on("vector", dve_khat)

                def dve_rotk(e, w=w):
                    kh3 = k_hat[:, :].rearrange("p (h d) -> p h d", d=DH)
                    rt3 = rtmp[:, :].rearrange("p (h d) -> p h d", d=DH)
                    e.tensor_tensor(
                        rt3[:, :, 0:32], kh3[:, :, 32:64],
                        Sk[:, 0:32].unsqueeze(1).broadcast_to((WS, H, 32)),
                        OP.mult)
                    e.tensor_tensor(
                        rt3[:, :, 32:64], kh3[:, :, 0:32],
                        Sk[:, 32:64].unsqueeze(1).broadcast_to((WS, H, 32)),
                        OP.mult)
                    e.tensor_tensor(
                        kh3, kh3,
                        Ck[:, :].unsqueeze(1).broadcast_to((WS, H, DH)),
                        OP.mult)
                    i = e.tensor_tensor(krot1[:, :], k_hat[:, :], rtmp[:, :],
                                        OP.add)
                    inc2(i, SV, f"krot_{w}")

                on("vector", dve_rotk)

                def pe_v(e, w=w):
                    wt(e, SV, f"xnTev_{w}")
                    if w <= 1:
                        wt(e, SGD, "setup_v")
                    if w >= 2:
                        wt(e, SA, f"sev_{w-2}")
                    elif w == 1:
                        wt(e, SV, "qhat_0")
                    for k in range(8):
                        for hf in range(2):
                            i = e.matmul(
                                pA[:, hf * 512:(hf + 1) * 512],
                                xnT[:, k, w * WS:(w + 1) * WS],
                                wqkv[:, k, 2 * D + hf * 512:
                                     2 * D + (hf + 1) * 512],
                                start=(k == 0), stop=(k == 7))
                    inc2(i, SP, f"pe_v_{w}")

                on("tensor", pe_v)

                def act_v(e, w=w):
                    wt(e, SP, f"pe_v_{w}")
                    if w >= 4:
                        wt(e, SP, f"pe_av_{w-3}_15")
                    i = e.copy(v_ring[:, w % 4, :].rearrange(
                                   "p (h c) -> p h c", c=65)[:, :, 0:64],
                               pA[:, :].rearrange("p (h d) -> p h d", d=DH))
                    inc2(i, SA, f"vev_{w}")

                on("scalar", act_v)

                def pe_q(e, w=w):
                    wt(e, SV, f"xnTev_{w}")
                    if w <= 1:
                        wt(e, SGD, "setup_q")
                    wt(e, SA, f"vev_{w}")
                    for k in range(8):
                        for hf in range(2):
                            i = e.matmul(
                                pA[:, hf * 512:(hf + 1) * 512],
                                xnT[:, k, w * WS:(w + 1) * WS],
                                wqkv[:, k, hf * 512:(hf + 1) * 512],
                                start=(k == 0), stop=(k == 7))
                    inc2(i, SP, f"pe_q_{w}")

                on("tensor", pe_q)

                def act_sqq(e, w=w):
                    wt(e, SP, f"pe_q_{w}")
                    if w >= 1:
                        wt(e, SV, f"redq_{w-1}")
                    i = e.activation(sq1[:, :], pA[:, :], AF.Square)
                    inc2(i, SA, f"sqq_{w}")

                on("scalar", act_sqq)

                def dve_redq(e, w=w):
                    wt(e, SA, f"sqq_{w}")
                    i = e.tensor_reduce(
                        n2[:, :], sq1[:, :].rearrange("p (h d) -> p h d", d=DH),
                        AX, OP.add)
                    inc(e, SV, f"redq_{w}")

                on("vector", dve_redq)

                def act_sqrtq(e, w=w):
                    wt(e, SV, f"redq_{w}")
                    e.activation(n2[:, :], n2[:, :], AF.Ln,
                                 bias=cvals[:, 1:2])
                    gap(e)
                    i = e.activation(rn[:, :], n2[:, :], AF.Exp, scale=-0.5)
                    inc(e, SA, f"sqrtq_{w}")

                on("scalar", act_sqrtq)

                def dve_qhat(e, w=w):
                    wt(e, SA, f"sqrtq_{w}")
                    i = e.tensor_tensor(
                        q_hat[:, :].rearrange("p (h d) -> p h d", d=DH),
                        pA[:, :].rearrange("p (h d) -> p h d", d=DH),
                        rn[:, :].unsqueeze(2).broadcast_to((128, H, DH)),
                        OP.mult)
                    inc2(i, SV, f"qhat_{w}")

                on("vector", dve_qhat)

                if w >= 1:
                    def pe_cr0(e, w=w):
                        wt(e, SV, f"khat_{w}")
                        wt(e, SA, f"kT1_{w-1}")
                        src = bkT[:, (w - 1) % 3, :, WS:2 * WS]
                        for nh in range(2):
                            e.matmul(pB[:, nh * 512:(nh + 1) * 512],
                                     R0Ts[:, 0, :],
                                     src[:, nh * 4:(nh + 1) * 4, :],
                                     start=True, stop=False)
                            i = e.matmul(pB[:, nh * 512:(nh + 1) * 512],
                                         R0Ts[:, 1, :],
                                         src[:, nh * 4:(nh + 1) * 4, :],
                                         start=False, stop=True)
                        inc2(i, SP, f"cr0mm_{w}")

                    on("tensor", pe_cr0)

                    def dve_cr0ev(e, w=w):
                        wt(e, SP, f"cr0mm_{w}")
                        if w >= 3:
                            wt(e, SP, f"pe_sim_{w-3}_15")
                        i = e.tensor_copy(
                            bkT[:, w % 3, :, 0:WS],
                            pB[:, 0:D].rearrange("p (a b) -> p a b", b=WS))
                        inc2(i, SV, f"cr0_{w}")

                    on("vector", dve_cr0ev)

                def pe_kT1(e, w=w):
                    wt(e, SV, f"krot_{w}")
                    wt(e, SV, f"xnTev_{w}")
                    for c in range(8):
                        i = e.transpose(pE[:, c * WS:(c + 1) * WS],
                                        krot1[:, c * WS:(c + 1) * WS],
                                        ident[:, :])
                    inc2(i, SP, f"pe_kT1_{w}")

                on("tensor", pe_kT1)

                def act_kT1(e, w=w):
                    wt(e, SP, f"pe_kT1_{w}")
                    if w >= 3:
                        wt(e, SP, f"pe_sim_{w-3}_15")
                    i = e.copy(bkT[:, w % 3, :, WS:2 * WS],
                               pE[:, :].rearrange("p (c n) -> p c n", n=WS))
                    inc2(i, SA, f"kT1_{w}")

                on("scalar", act_kT1)

                if w >= 1:
                    def pe_cr2(e, w=w):
                        wt(e, SA, f"kT1_{w}")
                        wt(e, SV, f"cr0_{w}")
                        src = bkT[:, w % 3, :, WS:2 * WS]
                        for nh in range(2):
                            e.matmul(pB[:, nh * 512:(nh + 1) * 512],
                                     R2Ts[:, 0, :],
                                     src[:, nh * 4:(nh + 1) * 4, :],
                                     start=True, stop=False)
                            i = e.matmul(pB[:, nh * 512:(nh + 1) * 512],
                                         R2Ts[:, 1, :],
                                         src[:, nh * 4:(nh + 1) * 4, :],
                                         start=False, stop=True)
                        inc2(i, SP, f"cr2mm_{w-1}")

                    on("tensor", pe_cr2)

                    def dve_cr2ev(e, w=w):
                        wt(e, SP, f"cr2mm_{w-1}")
                        if w >= 4:
                            wt(e, SP, f"pe_sim_{w-4}_15")
                        i = e.tensor_copy(
                            bkT[:, (w - 1) % 3, :, 2 * WS:3 * WS],
                            pB[:, 0:D].rearrange("p (a b) -> p a b", b=WS))
                        inc2(i, SV, f"cr2_{w-1}")

                    on("vector", dve_cr2ev)

                def dve_rotq(e, w=w):
                    if w >= 3:
                        wt(e, SP, f"pe_bqT_{w-3}")
                    qh3 = q_hat[:, :].rearrange("p (h d) -> p h d", d=DH)
                    e.tensor_tensor(
                        rtmp[:, :].rearrange("p (h d) -> p h d", d=DH)[:, :, 0:32],
                        qh3[:, :, 32:64],
                        Sq[:, 0:32].unsqueeze(1).broadcast_to((WS, H, 32)),
                        OP.mult)
                    e.tensor_tensor(
                        rtmp[:, :].rearrange("p (h d) -> p h d", d=DH)[:, :, 32:64],
                        qh3[:, :, 0:32],
                        Sq[:, 32:64].unsqueeze(1).broadcast_to((WS, H, 32)),
                        OP.mult)
                    e.tensor_tensor(
                        q_hat[:, :].rearrange("p (h d) -> p h d", d=DH),
                        qh3,
                        Cq[:, :].unsqueeze(1).broadcast_to((WS, H, DH)),
                        OP.mult)
                    i = e.tensor_tensor(bq[:, w % 3, :], q_hat[:, :],
                                        rtmp[:, :], OP.add)
                    inc2(i, SV, f"bq_{w}")

                on("vector", dve_rotq)

            for t in range(4 * w, min(4 * w + 4, len(WTILES))):
                emit_wtile(t)

        if TRUNC is None:
            EM["p1_pe"] = SP.c
            EM["p1_dve"] = SV.c
            EM["p1_act"] = SA.c
            EM["p1_pool"] = SG.c

            def p2_join(e):
                wt(e, SP, "p1_pe")
                wt(e, SV, "p1_dve")
                wt(e, SA, "p1_act")
                wt(e, SG, "p1_pool")

            p1.close()
            p2 = ExitStack()

            def sb2(name, shape, dt=F32):
                return sb(name, shape, dt, pool=p2)

            acc = sb2("acc", [128, 16, D])
            W1c = sb2("W1c", [128, 2, 8, 1024], BF16)
            W2c = sb2("W2c", [128, 8, 1024], BF16)
            hT = sb2("hT", [128, 8, 1024], BF16)
            wst2 = sb2("wst2", [128, 4, 1024])
            b1t_s = sb2("b1t_s", [128, 32])
            b2b_s = sb2("b2b_s", [128, D])
            g_t = sb2("g_t", [128, D])
            s2 = sb2("s2", [128, 2, D], BF16)
            x2 = sb2("x2", [128, 2, D])
            o2 = sb2("o2", [128, 2, D])
            tmagA = g_t[:, :]
            tmagB = o2[:, 0, :]

            f1 = [p2.enter_context(nc.psum_tensor(f"f1_{i}", [128, 512], F32))
                  for i in range(2)]
            f2 = [p2.enter_context(nc.psum_tensor(f"f2_{i}", [128, D], F32))
                  for i in range(3)]

            def sync_p2a(e):
                p2_join(e)
                sync_dma(e, b1t_s[:, :], b1t_d, SD)
                sync_dma(e, b2b_s[:, :], b2b_d, SD, "p2consts")

            on("sync", sync_p2a)

            def emit_scale(widx, part, outcol):
                def dve_a(e, part=part, widx=widx):
                    if widx == 1:
                        p2_join(e)
                    i = e.tensor_reduce(smallB[:, 0:1], part[:, :], AX, OP.add)
                    inc(e, SV, f"scred{widx}")

                on("vector", dve_a)

                def pe_a(e, widx=widx):
                    wt(e, SV, f"scred{widx}")
                    if widx == 1:
                        p2_join(e)
                    i = e.matmul(f1[0][0:1, 0:1], smallB[:, 0:1], onesc[:, :],
                                 start=True, stop=True)
                    inc(e, SP, f"scmm{widx}")

                on("tensor", pe_a)

                def dve_b(e, widx=widx):
                    wt(e, SP, f"scmm{widx}")
                    e.tensor_scalar(smallA[0:1, 2:3], f1[0][0:1, 0:1],
                                    1.0 / (D * FF), 1e-8, OP.mult, OP.max)
                    gap(e)
                    i = e.reciprocal(smallA[0:1, 3:4], smallA[0:1, 2:3])
                    inc(e, SV, f"screc{widx}")

                on("vector", dve_b)

                def pe_b(e, widx=widx):
                    wt(e, SV, f"screc{widx}")
                    i = e.matmul(f1[0][:, 1:2], onesr[:, :], smallA[0:1, 3:4],
                                 start=True, stop=True)
                    inc(e, SP, f"scbc{widx}")

                on("tensor", pe_b)

                def dve_c(e, widx=widx, outcol=outcol):
                    wt(e, SP, f"scbc{widx}")
                    i = e.tensor_copy(outcol[:, :], f1[0][:, 1:2])
                    inc(e, SV, f"rs{widx}")

                on("vector", dve_c)

            emit_scale(1, partials1, rs1col)
            emit_scale(2, partials2, rs2col)

            def emit_combine(tn):
                def gp_xs(e, tn=tn):
                    wt(e, SD, f"sst_{tn}")
                    if tn >= 2:
                        wt(e, SV, f"comb_{tn-2}")
                    pool_dma(e, x2[:, tn % 2, :], x_t4[tn], f"x2_{tn}")
                    pool_dma(e, s2[:, tn % 2, :], s_t4[tn], f"s2_{tn}")

                on("gpsimd", gp_xs)

                def act_silu(e, tn=tn):
                    wt(e, SV, f"accadd_3_{tn // 8}_{tn % 8}")
                    if tn >= 1:
                        wt(e, SV, f"comb_{tn-1}")
                    i = e.activation(g_t[:, :], acc[:, tn, :], AF.Silu)
                    inc2(i, SA, f"silu_{tn}")

                on("scalar", act_silu)

                def dve_comb(e, tn=tn):
                    wt(e, SA, f"silu_{tn}")
                    wt(e, SGD, f"s2_{tn}")
                    wt(e, SGD, f"x2_{tn}")
                    if tn >= 2:
                        wt(e, SO, f"ost_{tn-2}")
                    e.tensor_tensor(g_t[:, :], s2[:, tn % 2, :], g_t[:, :],
                                    OP.mult)
                    i = e.tensor_tensor(o2[:, tn % 2, :], x2[:, tn % 2, :],
                                        g_t[:, :], OP.add)
                    inc2(i, SV, f"comb_{tn}")

                on("vector", dve_comb)

                def sync_out(e, tn=tn):
                    wt(e, SV, f"comb_{tn}")
                    sync_dma(e, out_t4[tn], o2[:, tn % 2, :], SO, f"ost_{tn}")

                on("sync", sync_out)

            def wtile_src(c, which, k):
                return w1t_t[k, c] if which == 0 else w2t_t[c, k]

            def wtile_of(t):
                c, r = divmod(t, 16)
                which, k = divmod(r, 8)
                return c, which, k

            W1LIST = [c_ * 16 + k_ for c_ in range(4) for k_ in range(8)]
            W2LIST = [c_ * 16 + 8 + k_ for c_ in range(4) for k_ in range(8)]
            W1IDX = {t_: i_ for i_, t_ in enumerate(W1LIST)}
            W2IDX = {t_: i_ for i_, t_ in enumerate(W2LIST)}
            MAGIC = 12582912.0

            def emit_tern_core(t, slot, ready):
                c, r = divmod(t, 16)
                which, k = divmod(r, 8)
                dst = (W1c[:, c % 2] if which == 0 else W2c)
                rcol = rs1col if which == 0 else rs2col

                def dve_tern(e, t=t, slot=slot, dst=dst, rcol=rcol,
                             k=k, c=c, which=which, ready=ready):
                    wt(e, ready[0], ready[1])
                    if c == 0:
                        wt(e, SV, f"rs{1 + which}")
                    if which == 0:
                        if c >= 2:
                            wt(e, SP, f"f1_{c-2}_1_15")
                    else:
                        if c >= 1:
                            wt(e, SP, f"ffn2_{c-1}_1_7")
                    e.tensor_scalar(tmagA, wst2[:, slot, :],
                                    rcol[:, :], MAGIC, OP.mult, OP.add)
                    e.tensor_scalar(tmagB, tmagA, MAGIC, 1.0,
                                    OP.subtract, OP.min)
                    i = e.tensor_scalar(dst[:, k, :], tmagB,
                                        -1.0, None, OP.max)
                    inc2(i, SV, f"tern_{t}")

                on("vector", dve_tern)

            def emit_tern_tile(t):
                c, r = divmod(t, 16)
                which, k = divmod(r, 8)
                if which == 0:
                    i_ = W1IDX[t]

                    def gp_dma(e, t=t, i_=i_):
                        if i_ == 0:
                            p2_join(e)
                            pool_dma(e, wst2[:, 0, :],
                                     wtile_src(0, 0, 0), "w1st_0")
                        wt(e, SGD, f"w1st_{t}")
                        if i_ >= 1:
                            wt(e, SV, f"tern_{W1LIST[i_-1]}")
                        if i_ + 1 < 32:
                            nt = W1LIST[i_ + 1]
                            nc_, nw_, nk_ = wtile_of(nt)
                            pool_dma(e, wst2[:, (i_ + 1) % 2, :],
                                     wtile_src(nc_, nw_, nk_),
                                     f"w1st_{nt}")
                        if PASS["n"] == 2:
                            e.engine_nop().then_inc(SG.h, 1)
                        SG.c += 1
                        if PASS["n"] == 1:
                            EM[f"wrdy_{t}"] = SG.c
                        else:
                            assert EM[f"wrdy_{t}"] == SG.c

                    on("gpsimd", gp_dma)
                    emit_tern_core(t, W1IDX[t] % 2, (SG, f"wrdy_{t}"))
                else:
                    i_ = W2IDX[t]

                    def sync_w2(e, t=t, i_=i_, c=c, k=k):
                        if i_ >= 2:
                            wt(e, SV, f"tern_{W2LIST[i_-2]}")
                        sync_dma(e, wst2[:, 2 + i_ % 2, :],
                                 wtile_src(c, 1, k), SD, f"w2st_{t}")

                    on("sync", sync_w2)
                    emit_tern_core(t, 2 + W2IDX[t] % 2, (SD, f"w2st_{t}"))

            for t in range(16):
                emit_tern_tile(t)

            for c in range(4):
                cb = c % 2
                if c >= 1:
                    for t in range(c * 16 + 8, c * 16 + 16):
                        emit_tern_tile(t)
                for half in range(2):
                    for ffs in range(8):
                        for tq in range(2):
                            pi = (ffs * 2 + tq) % 2

                            def pe_f1(e, c=c, cb=cb, half=half, ffs=ffs, tq=tq,
                                      pi=pi):
                                if ffs == 0 and tq == 0:
                                    wt(e, SV, f"tern_{c * 16 + 7}")
                                    wt(e, SD, "p2consts")
                                idx = ffs * 2 + tq
                                if idx >= 2:
                                    wt(e, SA, f"gelu_{c}_{half}_{idx-2}")
                                tok0 = half * 1024 + tq * 512
                                for k in range(8):
                                    i = e.matmul(
                                        f1[pi][:, :],
                                        W1c[:, c % 2, k,
                                            ffs * 128:(ffs + 1) * 128],
                                        xnT[:, k, tok0:tok0 + 512],
                                        start=(k == 0), stop=(k == 7))
                                inc2(i, SP, f"f1_{c}_{half}_{idx}")

                            on("tensor", pe_f1)

                            def act_gelu(e, c=c, half=half, ffs=ffs, tq=tq,
                                         pi=pi):
                                idx = ffs * 2 + tq
                                wt(e, SP, f"f1_{c}_{half}_{idx}")
                                if c == 0 and half == 0 and idx == 0:
                                    wt(e, SD, "p2consts")
                                i = e.activation(
                                    hT[:, ffs, tq * 512:(tq + 1) * 512],
                                    f1[pi][:, :], AF.Gelu,
                                    bias=b1t_s[:, c * 8 + ffs:c * 8 + ffs + 1],
                                    scale=1.0)
                                inc2(i, SA, f"gelu_{c}_{half}_{idx}")

                            on("scalar", act_gelu)

                    if half == 1 and c + 1 < 4:
                        for t in range((c + 1) * 16, (c + 1) * 16 + 8):
                            emit_tern_tile(t)

                    for tt in range(8):
                        pi = tt % 3

                        def pe_f2(e, c=c, cb=cb, half=half, tt=tt, pi=pi):
                            if tt == 0:
                                wt(e, SA, f"gelu_{c}_{half}_15")
                                wt(e, SV, f"tern_{c * 16 + 15}")
                            if tt >= 3:
                                wt(e, SV, f"accadd_{c}_{half}_{tt-3}")
                            else:
                                pv = {0: 6, 1: 7, 2: 5}[tt]
                                if half == 1:
                                    wt(e, SV, f"accadd_{c}_0_{pv}")
                                elif c >= 1:
                                    wt(e, SV, f"accadd_{c-1}_1_{pv}")
                            for k in range(8):
                                for nh in range(2):
                                    i = e.matmul(
                                        f2[pi][:, nh * 512:(nh + 1) * 512],
                                        hT[:, k, tt * 128:(tt + 1) * 128],
                                        W2c[:, k, nh * 512:(nh + 1) * 512],
                                        start=(k == 0), stop=(k == 7))
                            inc2(i, SP, f"ffn2_{c}_{half}_{tt}")

                        on("tensor", pe_f2)

                        def dve_acc(e, c=c, half=half, tt=tt, pi=pi):
                            wt(e, SP, f"ffn2_{c}_{half}_{tt}")
                            if c == 0 and half == 0 and tt == 0:
                                wt(e, SD, "p2consts")
                            tn = half * 8 + tt
                            if c == 0:
                                i = e.tensor_tensor(acc[:, tn, :], f2[pi][:, :],
                                                    b2b_s[:, :], OP.add)
                            else:
                                i = e.tensor_tensor(acc[:, tn, :], acc[:, tn, :],
                                                    f2[pi][:, :], OP.add)
                            inc2(i, SV, f"accadd_{c}_{half}_{tt}")

                        on("vector", dve_acc)

                        if c == 3:
                            emit_combine(half * 8 + tt)

            def emit_taps():
                specs = {
                    "s_scr": (lambda: s_scr, ("SD", "sst_15")),
                    "xnT": (lambda: P["xnT"][:, :, :].rearrange(
                        "p a b -> p (a b)"), ("SA", "p1_act")),
                    "acc": (lambda: P["acc"][:, :, :].rearrange(
                        "p a b -> p (a b)"), ("SV", "accadd_3_1_7")),
                    "bkT": (lambda: P["bkT"][:, :, :, :].rearrange(
                        "p a b c -> p (a b c)"), ("SV", "accadd_3_1_7")),
                    "bq": (lambda: P["bq"][:, :, :].rearrange(
                        "p a b -> p (a b)"), ("SV", "accadd_3_1_7")),
                    "den": (lambda: P["den"][:, :, :].rearrange(
                        "p a b -> p (a b)"), ("SV", "accadd_3_1_7")),
                    "negb": (lambda: P["negb"][:, :, :].rearrange(
                        "p a b -> p (a b)"), ("SV", "accadd_3_1_7")),
                    "aoALL": (lambda: P["aoALL"][:, :], ("SV", "accadd_3_1_7")),
                    "krot1": (lambda: P["krot1"][:, :], ("SV", "accadd_3_1_7")),
                    "partials1": (lambda: P["partials1"][:, :],
                                  ("SV", "accadd_3_1_7")),
                    "partials2": (lambda: P["partials2"][:, :],
                                  ("SV", "accadd_3_1_7")),
                    "rs1col": (lambda: P["rs1col"][:, :], ("SV", "accadd_3_1_7")),
                    "rs2col": (lambda: P["rs2col"][:, :], ("SV", "accadd_3_1_7")),
                    "W1c": (lambda: P["W1c"][:, :, :].rearrange(
                        "p a b -> p (a b)"), ("SV", "accadd_3_1_7")),
                    "W2c": (lambda: P["W2c"][:, :, :].rearrange(
                        "p a b -> p (a b)"), ("SV", "accadd_3_1_7")),
                    "hT": (lambda: P["hT"][:, :, :].rearrange(
                        "p a b -> p (a b)"), ("SV", "accadd_3_1_7")),
                    "b1t_s": (lambda: P["b1t_s"][:, :], ("SV", "accadd_3_1_7")),
                    "den0": (lambda: P["den"][:, 0, :], ("SV", "ao_0")),
                    "negb0": (lambda: P["negb"][:, 0, :], ("SV", "negb_0")),
                    "p00": (lambda: P["p_sb"][:, 0, :], ("SA", "exp_0_0")),
                    "pT00": (lambda: P["pT_sb"][:, 0, :], ("SV", "pTev_0_0")),
                    "aoALL0": (lambda: P["aoALL"][:, :], ("SV", "ao_0")),
                    "bq0": (lambda: P["bq"][:, 0, :], ("SV", "bq_0")),
                    "bkT0": (lambda: P["bkT"][:, 0, :, :].rearrange(
                        "p a b -> p (a b)"), ("SP", "pe_sim_0_15")),
                    "stats0": (lambda: P["stats"][:, :], ("SV", "ln_0")),
                }
                sems = {"SD": SD, "SA": SA, "SV": SV, "SP": SP}
                for nm in tap_d:
                    fn, (semn, tag) = specs[nm]

                    def body(e, nm=nm, fn=fn, semn=semn, tag=tag):
                        wt(e, sems[semn], tag)
                        sync_dma(e, tap_d[nm], fn(), SO)

                    on("sync", body)

            emit_taps()
            EM["final_so"] = SO.c

            def sync_fin(e):
                e.wait_ge(SO.h, EM["final_so"])

            on("sync", sync_fin)

        else:
            def dbg_fin(e):
                if TRUNC >= 3:
                    wt(e, SD, f"sst_{TRUNC-3}")
                else:
                    wt(e, SC, "setup_sync")

            on("sync", dbg_fin)
            p1.close()

        PASS["n"] = 2
        for sem in (SD, SC, SGD, SP, SV, SA, SG, SO):
            sem.c = 0
        SYNC_RING["sd"] = 0
        SYNC_RING["so"] = 0
        for key in list(EM):
            pass
        bodies2 = list(BODIES)
        BODIES.clear()

        with nc.Block() as blk:
            def run_eng(engname):
                def closure(e):
                    for eng, body in bodies2:
                        if eng == engname:
                            body(e)
                return closure
            blk.sync(run_eng("sync"))
            blk.tensor(run_eng("tensor"))
            blk.vector(run_eng("vector"))
            blk.scalar(run_eng("scalar"))
            blk.gpsimd(run_eng("gpsimd"))

        if TRUNC is None:
            p2.close()

    return nc


_NC_CACHE = {}


def get_nc():
    if "nc" not in _NC_CACHE:
        _NC_CACHE["nc"] = build_nc()
    return _NC_CACHE["nc"]


def kernel(L0_attn_x, L0_attn_w_qkv, L0_attn_w_out, L0_attn_q_scale,
           L0_attn_k_scale, L0_ffn_W1, L0_ffn_b1, L0_ffn_W2, L0_ffn_b2):
    consts = host_consts()
    shared = host_inputs({
        "L0_attn_w_qkv": L0_attn_w_qkv, "L0_attn_w_out": L0_attn_w_out,
        "L0_attn_q_scale": L0_attn_q_scale, "L0_attn_k_scale": L0_attn_k_scale,
        "L0_ffn_W1": L0_ffn_W1, "L0_ffn_b1": L0_ffn_b1,
        "L0_ffn_W2": L0_ffn_W2, "L0_ffn_b2": L0_ffn_b2})
    x = np.ascontiguousarray(np.asarray(L0_attn_x, np.float32))
    base = {}
    base.update(consts)
    base.update(shared)
    in_maps = []
    for c in range(NCORE):
        m = dict(base)
        m["x"] = np.ascontiguousarray(x[c])
        in_maps.append(m)
    nc = get_nc()
    res = run_bass_kernel_spmd(nc, in_maps, core_ids=list(range(NCORE)))
    out = np.stack([res.results[c]["out"] for c in range(NCORE)], axis=0)
    return out


# version 40 (final)
# speedup vs baseline: 1.0233x; 1.0233x over previous
import numpy as np
import ml_dtypes
from contextlib import ExitStack

import concourse.bass as bass
import concourse.mybir as mybir
from concourse.alu_op_type import AluOpType as OP
from concourse.bass_utils import run_bass_kernel_spmd

F32 = mybir.dt.float32
BF16 = mybir.dt.bfloat16
I32 = mybir.dt.int32
AF = mybir.ActivationFunctionType
AX = mybir.AxisListType.X
BF = ml_dtypes.bfloat16

SEQ, D, H, DH, NW, WS, FF = 2048, 1024, 16, 64, 16, 128, 4096
NEG = -1.0e9
NCORE = 8
TRUNC = None


def host_consts():
    c = {}
    c["ident"] = np.eye(WS, dtype=BF)
    invf = (1.0 / 10000.0 ** (np.arange(0, DH, 2, dtype=np.float64) / DH))
    fr = invf[np.arange(DH) % 32]
    sign = np.where(np.arange(DH) < 32, -1.0, 1.0)

    def rot_tables(pos0, scale):
        ang = (pos0 + np.arange(WS))[:, None] * fr[None, :]
        cos = (scale * np.cos(ang)).astype(np.float32)
        sin = (scale * np.sin(ang) * sign[None, :]).astype(np.float32)
        return cos, sin
    c["cosq"], c["sinq"] = rot_tables(2 * WS, 8.0)
    c["cosk"], c["sink"] = rot_tables(WS, 1.0)
    d_of_p = np.arange(128) % DH
    delta = WS * fr[d_of_p]
    c["cD"] = np.cos(delta).astype(np.float32)[:, None]
    sD = (np.sin(delta) * sign[d_of_p])[:, None]
    c["sDp"] = sD.astype(np.float32)
    c["sDm"] = (-sD).astype(np.float32)
    sig_p = np.where((d_of_p % DH) < 32, np.arange(128) + 32,
                     np.arange(128) - 32)
    R2 = np.zeros((128, 128), np.float64)
    R2[np.arange(128), np.arange(128)] = np.cos(delta)
    R2[np.arange(128), sig_p] = sD[:, 0]
    R0 = np.zeros((128, 128), np.float64)
    R0[np.arange(128), np.arange(128)] = np.cos(delta)
    R0[np.arange(128), sig_p] = -sD[:, 0]
    for nm, R in (("R0T", R0.T), ("R2T", R2.T)):
        hi = np.ascontiguousarray(R).astype(BF)
        lo = (R - hi.astype(np.float64)).astype(BF)
        c[nm + "h"] = hi
        c[nm + "l"] = np.ascontiguousarray(lo)
    kc = np.arange(WS)[:, None]
    qq = np.arange(WS)[None, :]
    tri0 = (kc >= qq).astype(np.float64)
    tri2 = (kc <= qq).astype(np.float64)
    c["maskcat"] = np.concatenate([tri0, tri2], axis=1).astype(BF)
    vones = np.zeros((128, 4, H, 65), np.float64)
    vones[:, :, :, 64] = 1.0
    c["vones"] = vones.reshape(128, 4 * H * 65).astype(BF)
    c["onesr"] = np.ones((1, 128), dtype=np.float32)
    c["onesc"] = np.ones((128, 1), dtype=np.float32)
    c["cvals"] = np.broadcast_to(
        np.array([1e-5, 1e-24, 1e-20, 1e-30], np.float32), (128, 4)).copy()
    return c


SIG = np.r_[np.arange(32, 64), np.arange(0, 32)]


def host_inputs(inp):
    d = {}
    d["wqkv"] = np.ascontiguousarray(inp["L0_attn_w_qkv"], np.float32)
    d["wout"] = np.ascontiguousarray(inp["L0_attn_w_out"], np.float32)
    qs = np.asarray(inp["L0_attn_q_scale"], np.float32)
    ks = np.asarray(inp["L0_attn_k_scale"], np.float32)
    d["qs_t"] = np.ascontiguousarray(np.broadcast_to(qs, (WS, DH)))
    d["qs_s"] = np.ascontiguousarray(np.broadcast_to(qs[SIG], (WS, DH)))
    d["ks_t"] = np.ascontiguousarray(np.broadcast_to(ks, (WS, DH)))
    d["ks_s"] = np.ascontiguousarray(np.broadcast_to(ks[SIG], (WS, DH)))
    d["w1t"] = np.ascontiguousarray(np.asarray(inp["L0_ffn_W1"], np.float32).T)
    d["w2t"] = np.ascontiguousarray(np.asarray(inp["L0_ffn_W2"], np.float32).T)
    d["b1t"] = np.ascontiguousarray(
        np.asarray(inp["L0_ffn_b1"], np.float32).reshape(32, 128).T)
    d["b2b"] = np.ascontiguousarray(
        np.broadcast_to(np.asarray(inp["L0_ffn_b2"], np.float32), (128, D)))
    return d


def build_nc(taps=()):
    nc = bass.Bass("TRN2", target_bir_lowering=False)
    EM = {}

    def din(name, shape, dt=F32):
        return nc.dram_tensor(name, shape, dt, kind="ExternalInput").ap()

    x_d = din("x", [SEQ, D])
    wqkv_d = din("wqkv", [D, 3 * D])
    wout_d = din("wout", [D, D])
    qs_t_d = din("qs_t", [WS, DH]); qs_s_d = din("qs_s", [WS, DH])
    ks_t_d = din("ks_t", [WS, DH]); ks_s_d = din("ks_s", [WS, DH])
    w1t_d = din("w1t", [D, FF])
    w2t_d = din("w2t", [FF, D])
    b1t_d = din("b1t", [128, 32])
    b2b_d = din("b2b", [128, D])
    ident_d = din("ident", [WS, WS], BF16)
    cosq_d = din("cosq", [WS, DH]); sinq_d = din("sinq", [WS, DH])
    cosk_d = din("cosk", [WS, DH]); sink_d = din("sink", [WS, DH])
    cD_d = din("cD", [128, 1]); sDp_d = din("sDp", [128, 1])
    sDm_d = din("sDm", [128, 1])
    R0Th_d = din("R0Th", [128, 128], BF16)
    R0Tl_d = din("R0Tl", [128, 128], BF16)
    R2Th_d = din("R2Th", [128, 128], BF16)
    R2Tl_d = din("R2Tl", [128, 128], BF16)
    maskcat_d = din("maskcat", [128, 2 * WS], BF16)
    vones_d = din("vones", [128, 4 * H * 65], BF16)
    onesr_d = din("onesr", [1, 128]); onesc_d = din("onesc", [128, 1])
    cvals_d = din("cvals", [128, 4])

    out_d = nc.dram_tensor("out", [SEQ, D], F32, kind="ExternalOutput").ap()
    s_scr = nc.dram_tensor("s_scr", [SEQ, D], BF16, kind="Internal").ap()

    tap_d = {}
    for tname, tshape, tdt in taps:
        tap_d[tname] = nc.dram_tensor("tap_" + tname, tshape, tdt,
                                      kind="ExternalOutput").ap()

    x_t4 = x_d.rearrange("(w p) n -> w p n", p=WS)
    s_t4 = s_scr.rearrange("(w p) n -> w p n", p=WS)
    out_t4 = out_d.rearrange("(w p) n -> w p n", p=WS)
    wqkv_t = wqkv_d.rearrange("(k p) n -> k p n", p=WS)
    wout_t = wout_d.rearrange("(k p) n -> k p n", p=WS)
    w1t_t = w1t_d.rearrange("(k p) (c n) -> k c p n", p=WS, n=1024)
    w2t_t = w2t_d.rearrange("(c k p) n -> c k p n", p=WS, k=8)

    class S:
        def __init__(self, name):
            self.h = stack.enter_context(nc.semaphore(name))
            self.c = 0

    stack = ExitStack()
    SD = None

    class _DummyInst:
        def then_inc(self, *a, **k):
            return self

    class _Proxy:
        def __getattr__(self, name):
            def f(*a, **k):
                return _DummyInst()
            return f

    BODIES = []
    PASS = {"n": 1}

    def on(eng, body):
        BODIES.append((eng, body))
        body(_Proxy())

    def inc2(inst, sem, tag=None, n=1):
        if PASS["n"] == 2:
            inst.then_inc(sem.h, n)
        sem.c += n
        if tag is not None:
            if PASS["n"] == 1:
                assert tag not in EM, tag
                EM[tag] = sem.c
            else:
                assert EM[tag] == sem.c, (tag, EM[tag], sem.c)

    def incm2(inst, sem, tags, n=None):
        n = len(tags) if n is None else n
        if PASS["n"] == 2:
            inst.then_inc(sem.h, n)
        sem.c += n
        for tag in tags:
            if PASS["n"] == 1:
                assert tag not in EM, tag
                EM[tag] = sem.c
            else:
                assert EM[tag] == sem.c, (tag, EM[tag], sem.c)

    def incm(e, sem, tags, n=None):
        n = len(tags) if n is None else n
        if PASS["n"] == 2:
            e.drain().then_inc(sem.h, n)
        sem.c += n
        for tag in tags:
            if PASS["n"] == 1:
                assert tag not in EM, tag
                EM[tag] = sem.c
            else:
                assert EM[tag] == sem.c, (tag, EM[tag], sem.c)

    def inc(e, sem, tag=None, n=1):
        if PASS["n"] == 2:
            e.drain().then_inc(sem.h, n)
        sem.c += n
        if tag is not None:
            if PASS["n"] == 1:
                assert tag not in EM, tag
                EM[tag] = sem.c
            else:
                assert EM[tag] == sem.c, (tag, EM[tag], sem.c)

    SYNC_RING = {"sd": 0, "so": 0}

    def sync_dma(e, out_ap, in_ap, sem, tag=None):
        if PASS["n"] == 2:
            if SYNC_RING["sd"] + SYNC_RING["so"] > 0:
                e.wait_ge(SD.h, SYNC_RING["sd"] * 16)
                if SYNC_RING["so"]:
                    e.wait_ge(SO.h, SYNC_RING["so"] * 16)
            e.dma_start(out_ap, in_ap).then_inc(sem.h, 16)
        sem.c += 16
        if sem is SD:
            SYNC_RING["sd"] = sem.c // 16
        else:
            SYNC_RING["so"] = sem.c // 16
        if tag is not None:
            if PASS["n"] == 1:
                assert tag not in EM, tag
                EM[tag] = sem.c
            else:
                assert EM[tag] == sem.c, (tag, EM[tag], sem.c)

    POOL_RING = {"n": 0}

    def pool_dma(e, out_ap, in_ap, tag=None):
        if PASS["n"] == 2:
            e.wait_ge(SGD.h, SGD.c)
            e.dma_start(out_ap, in_ap).then_inc(SGD.h, 16)
        SGD.c += 16
        if tag is not None:
            if PASS["n"] == 1:
                assert tag not in EM, tag
                EM[tag] = SGD.c
            else:
                assert EM[tag] == SGD.c, (tag, EM[tag], SGD.c)

    def gap(e):
        if PASS["n"] == 2:
            e.drain()

    def wt(e, sem, tag):
        if PASS["n"] == 1:
            assert tag in EM, f"wait on not-yet-produced mark {tag}"
        else:
            e.wait_ge(sem.h, EM[tag])

    with stack:
        SD = S("SD")
        SC = S("SC")
        SGD = S("SGD")
        SP = S("SP")
        SV = S("SV")
        SA = S("SA")
        SG = S("SG")
        SO = S("SO")
        MASK_ENG, MASK_SEM = "gpsimd", SG
        EMIT_MASKS = True

        P = {}

        def sb(name, shape, dt=F32, pool=stack):
            P[name] = pool.enter_context(nc.sbuf_tensor("sb_" + name, shape,
                                                        dt))
            return P[name]

        xnT = sb("xnT", [128, 8, SEQ], BF16)
        ident = sb("ident", [WS, WS], BF16)
        onesr = sb("onesr", [1, 128]); onesc = sb("onesc", [128, 1])
        partials1 = sb("partials1", [128, 32])
        partials2 = sb("partials2", [128, 32])
        rs1col = sb("rs1col", [128, 1]); rs2col = sb("rs2col", [128, 1])
        smallA = sb("smallA", [1, 8])
        smallB = sb("smallB", [128, 1])
        cvals = sb("cvals", [128, 4])

        p1 = ExitStack()

        def sb1(name, shape, dt=F32):
            return sb(name, shape, dt, pool=p1)

        wqkv = sb1("wqkv", [128, 8, 3 * D], BF16)
        woutW = sb1("woutW", [128, 8, D], BF16)
        x_t = sb1("x_t", [128, 2, D])
        xn_t = sb1("xn_t", [128, 2, D], BF16)
        sq1 = sb1("sq1", [128, D]); sq2 = sb1("sq2", [128, D])
        q_hat = sb1("q_hat", [128, D])
        k_hat = sb1("k_hat", [128, D])
        rtmp = sb1("rtmp", [128, D])
        krot1 = sb1("krot1", [128, D], BF16)
        bq = sb1("bq", [128, 3, D], BF16)
        bqT = sb1("bqT", [128, 2, 8, WS], BF16)
        bkT = sb1("bkT", [128, 3, 8, 3 * WS], BF16)
        R0Ts = sb1("R0Ts", [128, 2, WS], BF16)
        R2Ts = sb1("R2Ts", [128, 2, WS], BF16)
        v_ring = sb1("v_ring", [128, 4, H * 65], BF16)
        pTall = sb1("pTall", [128, 4, 3 * WS], BF16)
        aoALL = sb1("aoALL", [128, D], BF16)
        aoT = sb1("aoT", [128, 8, WS], BF16)
        s_tile = sb1("s_tile", [128, 2, D], BF16)
        stats = sb1("stats", [128, 8])
        n2 = sb1("n2", [128, H]); rn = sb1("rn", [128, H])
        den = sb1("den", [128, 2, H]); rden = sb1("rden", [128, 2, H])
        Cq = sb1("Cq", [WS, DH]); Sq = sb1("Sq", [WS, DH])
        Ck = sb1("Ck", [WS, DH]); Sk = sb1("Sk", [WS, DH])
        qs_sb = sb1("qs_sb", [WS, DH]); qss_sb = sb1("qss_sb", [WS, DH])
        ks_sb = sb1("ks_sb", [WS, DH]); kss_sb = sb1("kss_sb", [WS, DH])
        tbl = sb1("tbl", [WS, 4, DH])
        maskcat = sb1("maskcat", [128, 2 * WS], BF16)
        negBcol = sb1("negBcol", [128, 1])
        wst = sb1("wst", [128, 4, 1024])

        pA = p1.enter_context(nc.psum_tensor("pA", [128, D], F32))
        pB = p1.enter_context(nc.psum_tensor("pB", [128, 1536], F32))
        pC = p1.enter_context(nc.psum_tensor("pC", [128, 1024], F32))
        pE = p1.enter_context(nc.psum_tensor("pE", [128, D], BF16))

        def av_dst(h):
            if h < 7:
                off = 65 * h
            elif h < 14:
                off = 512 + 65 * (h - 7)
            else:
                off = 1024 + 65 * (h - 14)
            return pB[:, off:off + 65]

        AVG = [(0, 7, 0), (512, 7, 7), (1024, 2, 14)]
        CPERM = {0: 0, 1: 2 * WS, 2: WS}

        def setup_sync(e):
            e.dma_start(ident[:, :], ident_d).then_inc(SC.h, 16)
            e.dma_start(onesr[:, :], onesr_d).then_inc(SC.h, 16)
            e.dma_start(onesc[:, :], onesc_d).then_inc(SC.h, 16)
            e.dma_start(cvals[:, :], cvals_d).then_inc(SC.h, 16)
            e.dma_start(tbl[:, 0, :], cosq_d).then_inc(SC.h, 16)
            e.dma_start(tbl[:, 1, :], sinq_d).then_inc(SC.h, 16)
            e.dma_start(tbl[:, 2, :], cosk_d).then_inc(SC.h, 16)
            e.dma_start(tbl[:, 3, :], sink_d).then_inc(SC.h, 16)
            e.dma_start(qs_sb[:, :], qs_t_d).then_inc(SC.h, 16)
            e.dma_start(qss_sb[:, :], qs_s_d).then_inc(SC.h, 16)
            e.dma_start(ks_sb[:, :], ks_t_d).then_inc(SC.h, 16)
            e.dma_start(kss_sb[:, :], ks_s_d).then_inc(SC.h, 16)
            e.dma_start(R0Ts[:, 0, :], R0Th_d).then_inc(SC.h, 16)
            e.dma_start(R0Ts[:, 1, :], R0Tl_d).then_inc(SC.h, 16)
            e.dma_start(R2Ts[:, 0, :], R2Th_d).then_inc(SC.h, 16)
            e.dma_start(R2Ts[:, 1, :], R2Tl_d).then_inc(SC.h, 16)
            e.dma_start(maskcat[:, :], maskcat_d).then_inc(SC.h, 16)
            e.dma_start(v_ring[:, :, :],
                        vones_d.rearrange("p (a b) -> p a b",
                                          b=H * 65)).then_inc(SC.h, 16)
            SC.c += 16 * 18
            EM["setup_sync"] = SC.c

        on("sync", setup_sync)

        def setup_gp(e):
            for lo, hi, tag in ((D, 2 * D, "setup_k"),
                                (2 * D, 3 * D, "setup_v"),
                                (0, D, "setup_q")):
                for k in range(8):
                    e.dma_start(wqkv[:, k, lo:hi],
                                wqkv_t[k][:, lo:hi]).then_inc(SGD.h, 16)
                SGD.c += 16 * 8
                EM[tag] = SGD.c
                e.wait_ge(SGD.h, SGD.c)
            for k in range(8):
                e.dma_start(woutW[:, k, :], wout_t[k]).then_inc(SGD.h, 16)
            SGD.c += 16 * 8
            EM["setup_gp"] = SGD.c

        on("gpsimd", setup_gp)

        def setup_dve(e):
            wt(e, SC, "setup_sync")
            e.tensor_tensor(Cq[:, :], tbl[:, 0, :], qs_sb[:, :], OP.mult)
            e.tensor_tensor(Sq[:, :], tbl[:, 1, :], qss_sb[:, :], OP.mult)
            e.tensor_tensor(Ck[:, :], tbl[:, 2, :], ks_sb[:, :], OP.mult)
            e.tensor_tensor(Sk[:, :], tbl[:, 3, :], kss_sb[:, :], OP.mult)
            e.tensor_reduce(smallA[0:1, 0:1], ks_sb[0:1, :], AX, OP.max,
                            apply_absolute_value=True)
            e.tensor_reduce(smallA[0:1, 4:5], qs_sb[0:1, :], AX, OP.max,
                            apply_absolute_value=True)
            gap(e)
            e.tensor_tensor(smallA[0:1, 5:6], smallA[0:1, 0:1],
                            smallA[0:1, 4:5], OP.mult)
            gap(e)
            i2 = e.tensor_scalar(smallA[0:1, 1:2], smallA[0:1, 5:6],
                                 -8.0, None, OP.mult)
            inc(e, SV, "setup_dve")

        on("vector", setup_dve)

        def setup_pe(e):
            wt(e, SV, "setup_dve")
            i = e.matmul(pC[:, 0:1], onesr[:, :], smallA[0:1, 1:2],
                         start=True, stop=True)
            inc(e, SP, "setup_pe_mk2")

        on("tensor", setup_pe)

        def setup_dve2(e):
            wt(e, SP, "setup_pe_mk2")
            i = e.tensor_copy(negBcol[:, :], pC[:, 0:1])
            inc(e, SV, "setup_done_dve")

        on("vector", setup_dve2)

        def tap(name, ap_fn, wait=None):
            if name not in tap_d:
                return

            def body(e):
                if wait is not None:
                    sem, tag = wait
                    wt(e, sem, tag)
                sync_dma(e, tap_d[name], ap_fn(), SO, "tap_" + name)

            on("sync", body)

        vslot = {w: (w % 4) for w in range(-1, NW + 1)}
        vslot[-1] = 4

        W1TILES = [("w1", k, cc) for k in range(8) for cc in range(4)]
        W2TILES = [("w2", c, k) for c in range(4) for k in range(8)]
        WTILES = W1TILES + W2TILES

        for w in range((NW + 2) if TRUNC is None else TRUNC):
            qw = w - 2
            hasA = w <= NW - 1
            hasB = qw >= 0

            def sync_a(e, w=w):
                for v in ([0, 1] if w == 0 else [w + 1]):
                    if v > NW - 1:
                        continue
                    if v >= 2:
                        wt(e, SV, f"ln_{v-2}")
                    sync_dma(e, x_t[:, v % 2, :], x_t4[v], SD, f"x_{v}")
            on("sync", sync_a)

            def emit_ln_var(v):
                def dve_ln(e, v=v):
                    wt(e, SD, f"x_{v}")
                    xt = x_t[:, v % 2, :]
                    e.tensor_reduce(stats[:, 0:1], xt, AX, OP.add)
                    gap(e)
                    e.tensor_scalar(stats[:, 1:2], stats[:, 0:1],
                                    -1.0 / D, None, OP.mult)
                    gap(e)
                    i = e.scalar_tensor_tensor(sq1[:, :], xt, stats[:, 1:2],
                                               xt, OP.add, OP.mult,
                                               accum_out=stats[:, 2:3])
                    inc(e, SV, f"lnvar_{v}")

                on("vector", dve_ln)

            def emit_ln_rest(v):
                def act_ln0(e, v=v):
                    wt(e, SV, f"lnvar_{v}")
                    e.activation(stats[:, 3:4], stats[:, 2:3], AF.Ln,
                                 bias=cvals[:, 0:1], scale=1.0 / D)
                    gap(e)
                    i = e.activation(stats[:, 4:5], stats[:, 3:4],
                                     AF.Exp, scale=-0.5)
                    inc(e, SA, f"lnsd_{v}")

                on("scalar", act_ln0)

                def dve_ln20(e, v=v):
                    wt(e, SA, f"lnsd_{v}")
                    if v >= 2:
                        wt(e, SP, f"pe_xnT_{v-2}")
                    i = e.tensor_scalar(xn_t[:, v % 2, :],
                                        x_t[:, v % 2, :],
                                        stats[:, 1:2], stats[:, 4:5],
                                        OP.add, OP.mult)
                    inc2(i, SV, f"ln_{v}")

                on("vector", dve_ln20)

            if w == 0:
                emit_ln_var(0)
                emit_ln_rest(0)
            if w + 1 <= NW - 1:
                emit_ln_var(w + 1)
                if w < 2:
                    emit_ln_rest(w + 1)



            def emit_bqT_early(w=w):
                nqw = w - 1

                def pe_bqT(e, nqw=nqw, w=w):
                    wt(e, SV, f"bq_{nqw}")
                    wt(e, SA, f"kT1_{nqw}")
                    for c in range(8):
                        i = e.transpose(pE[:, c * WS:(c + 1) * WS],
                                        bq[:, nqw % 3, c * WS:(c + 1) * WS],
                                        ident[:, :])
                    inc2(i, SP, f"pe_bqT_{nqw}")

                on("tensor", pe_bqT)

                def dve_bqT(e, nqw=nqw):
                    wt(e, SP, f"pe_bqT_{nqw}")
                    i = e.tensor_copy(bqT[:, nqw % 2, :, :],
                                      pE[:, :].rearrange(
                                          "p (c n) -> p c n", n=WS))
                    inc2(i, SV, f"bqTev_{nqw}")

                on("vector", dve_bqT)

            if w == 1:
                emit_bqT_early()
            if hasB:
                c_list = ([1, 2] if qw == 0 else
                          ([0, 1] if qw == NW - 1 else [0, 1, 2]))

                def emit_sim(h, qw=qw, c_list=c_list):
                    def pe_sim(e, h=h):
                        if h == 0:
                            if qw == 0:
                                wt(e, SV, "setup_done_dve")
                            if qw == 0:
                                wt(e, SA, f"kT1_{qw}")
                            if qw <= NW - 2:
                                wt(e, SV, f"cr2_{qw}")
                            else:
                                wt(e, SV, f"bqTev_{qw}")
                        if h >= 2:
                            wt(e, SA, f"exp_{qw}_{h-2}")
                        elif qw >= 1:
                            wt(e, SA, f"exp_{qw-1}_{h+14}")
                        ph = (h % 2) * DH
                        ch = h // 2
                        b = (h % 2) * 512
                        for c in c_list:
                            i = e.matmul(
                                pC[:, b + CPERM[c]:b + CPERM[c] + WS],
                                bkT[ph:ph + DH, qw % 3, ch,
                                    c * WS:(c + 1) * WS],
                                bqT[ph:ph + DH, qw % 2, ch, :],
                                start=True, stop=True)
                        LASTI[0] = i

                    on("tensor", pe_sim)

                def emit_exp(h, qw=qw):
                    def act_exp(e, h=h):
                        wt(e, SP, f"pe_sim_{qw}_{h}")
                        if qw == 0 and h == 0:
                            wt(e, SV, "setup_done_dve")
                        b = (h % 2) * 512
                        s = h % 4
                        if qw == 0:
                            i = e.activation(pTall[:, s, WS:3 * WS],
                                             pC[:, b + WS:b + 3 * WS],
                                             AF.Exp,
                                             bias=negBcol[:, 0:1], scale=1.0)
                        elif qw == NW - 1:
                            e.activation(pTall[:, s, 0:WS],
                                         pC[:, b:b + WS], AF.Exp,
                                         bias=negBcol[:, 0:1], scale=1.0)
                            i = e.activation(pTall[:, s, 2 * WS:3 * WS],
                                             pC[:, b + 2 * WS:b + 3 * WS],
                                             AF.Exp,
                                             bias=negBcol[:, 0:1], scale=1.0)
                        else:
                            i = e.activation(pTall[:, s, :],
                                             pC[:, b:b + 3 * WS], AF.Exp,
                                             bias=negBcol[:, 0:1], scale=1.0)
                        inc2(i, SA, f"exp_{qw}_{h}")

                    on("scalar", act_exp)

                def emit_mask(h, qw=qw):
                    def gp_mask(e, h=h):
                        wt(e, SA, f"exp_{qw}_{h}")
                        s = h % 4
                        if qw == 0:
                            lo, hi = WS, 2 * WS
                        elif qw == NW - 1:
                            lo, hi = 0, WS
                        else:
                            lo, hi = 0, 2 * WS
                        i = e.tensor_tensor(pTall[:, s, lo:hi],
                                            pTall[:, s, lo:hi],
                                            maskcat[:, lo:hi], OP.mult)
                        inc2(i, MASK_SEM, f"mask_{qw}_{h}")

                    on(MASK_ENG, gp_mask)

                def emit_av(h, qw=qw, c_list=c_list):
                    def pe_av(e, h=h):
                        if EMIT_MASKS:
                            wt(e, MASK_SEM, f"mask_{qw}_{h}")
                        else:
                            wt(e, SA, f"exp_{qw}_{h}")
                        if h == 0:
                            if qw <= NW - 2:
                                wt(e, SV, f"cr2_{qw}")
                            else:
                                wt(e, SV, f"ao_{qw-1}")
                            wt(e, SA, f"vev_{min(qw + 1, NW - 1)}")
                        s = h % 4
                        first, last = c_list[0], c_list[-1]
                        for c in c_list:
                            slot = (qw - 1 + c) % 4
                            i = e.matmul(
                                av_dst(h),
                                pTall[:, s, CPERM[c]:CPERM[c] + WS],
                                v_ring[:, slot, h * 65:(h + 1) * 65],
                                start=(c == first), stop=(c == last))
                        LASTI[0] = i

                    on("tensor", pe_av)

                LASTI = [None]
                for h in range(H + 3):
                    tags = []
                    if h < H:
                        emit_sim(h)
                        tags.append(f"pe_sim_{qw}_{h}")
                    if h >= 3:
                        emit_av(h - 3)
                        tags.append(f"pe_av_{qw}_{h-3}")

                    def pe_hmark(e, tags=tuple(tags)):
                        incm2(LASTI[0], SP, list(tags))

                    on("tensor", pe_hmark)
                    if h < H:
                        emit_exp(h)
                        if EMIT_MASKS:
                            emit_mask(h)

                if w <= NW:
                    emit_bqT_early()

                def dve_dengath(e, qw=qw):
                    wt(e, SP, f"pe_av_{qw}_15")
                    if qw >= 2:
                        wt(e, SV, f"rden_{qw-2}")
                    for off, nh, h0 in AVG:
                        i = e.tensor_scalar(
                            den[:, qw % 2, h0:h0 + nh].unsqueeze(2),
                            pB[:, off:off + nh * 65].rearrange(
                                "p (h c) -> p h c", c=65)[:, :, 64:65],
                            1e-30, None, OP.add)
                    inc2(i, SV, f"dengath_{qw}")

                on("vector", dve_dengath)

                def dve_rden(e, qw=qw):
                    if qw >= 2:
                        wt(e, SV, f"ao_{qw-2}")
                    gap(e)
                    i = e.reciprocal(rden[:, qw % 2, :], den[:, qw % 2, :])
                    inc(e, SV, f"rden_{qw}")

                on("vector", dve_rden)

                def dve_ao(e, qw=qw):
                    wt(e, SV, f"rden_{qw}")
                    if qw >= 1:
                        wt(e, SP, f"pe_aoT_{qw-1}")
                    for off, nh, h0 in AVG:
                        i = e.tensor_tensor(
                            aoALL[:, h0 * DH:(h0 + nh) * DH].rearrange(
                                "p (h d) -> p h d", d=DH),
                            pB[:, off:off + nh * 65].rearrange(
                                "p (h c) -> p h c", c=65)[:, :, 0:64],
                            rden[:, qw % 2, h0:h0 + nh].unsqueeze(2)
                                .broadcast_to((128, nh, DH)),
                            OP.mult)
                    inc2(i, SV, f"ao_{qw}")

                on("vector", dve_ao)

                if w + 1 <= NW - 1:
                    emit_ln_rest(w + 1)

                def pe_aoT(e, qw=qw):
                    wt(e, SV, f"ao_{qw}")
                    wt(e, SV, f"bqTev_{min(qw + 1, NW - 1)}")
                    for c in range(8):
                        i = e.transpose(pE[:, c * WS:(c + 1) * WS],
                                        aoALL[:, c * WS:(c + 1) * WS],
                                        ident[:, :])
                    inc2(i, SP, f"pe_aoT_{qw}")

                on("tensor", pe_aoT)

                def act_aoT(e, qw=qw):
                    wt(e, SP, f"pe_aoT_{qw}")
                    if qw >= 1:
                        wt(e, SP, f"pe_wout_{qw-1}")
                    i = e.copy(aoT[:, :, :],
                               pE[:, :].rearrange("p (c n) -> p c n", n=WS))
                    inc2(i, SA, f"aoTev_{qw}")

                on("scalar", act_aoT)

                def pe_wout(e, qw=qw, w=w):
                    wt(e, SA, f"aoTev_{qw}")
                    if qw <= 1:
                        wt(e, SGD, "setup_gp")
                    wt(e, SV, f"qhat_{min(w - 1, NW - 1)}")
                    for k in range(8):
                        for hf in range(2):
                            i = e.matmul(pA[:, hf * 512:(hf + 1) * 512],
                                         aoT[:, k, :],
                                         woutW[:, k, hf * 512:(hf + 1) * 512],
                                         start=(k == 0), stop=(k == 7))
                    inc2(i, SP, f"pe_wout_{qw}")

                on("tensor", pe_wout)

                def act_s(e, qw=qw):
                    wt(e, SP, f"pe_wout_{qw}")
                    if qw >= 2:
                        wt(e, SD, f"sst_{qw-2}")
                    i = e.copy(s_tile[:, qw % 2, :], pA[:, :])
                    inc2(i, SA, f"sev_{qw}")

                on("scalar", act_s)

                def sync_s(e, qw=qw):
                    wt(e, SA, f"sev_{qw}")
                    sync_dma(e, s_t4[qw], s_tile[:, qw % 2, :], SD,
                             f"sst_{qw}")

                on("sync", sync_s)

            def emit_wtile(t, w=w):
                kind, a, b = WTILES[t]

                def gp_wst(e, t=t, kind=kind, a=a, b=b):
                    if t >= 4:
                        sem = SV if ((t - 4) % 2 == 0) else SA
                        wt(e, sem, f"absred_{t-4}")
                    src = w1t_t[a, b] if kind == "w1" else w2t_t[a, b]
                    pool_dma(e, wst[:, t % 4, :], src, f"wst_{t}")

                on("gpsimd", gp_wst)

                part = partials1 if kind == "w1" else partials2
                col = (t - (0 if kind == "w1" else 32))
                if t % 2 == 0:
                    def dve_abs(e, t=t, part=part, col=col):
                        wt(e, SGD, f"wst_{t}")
                        i = e.tensor_reduce(part[:, col:col + 1],
                                            wst[:, t % 4, :], AX, OP.add,
                                            apply_absolute_value=True)
                        inc2(i, SV, f"absred_{t}")

                    on("vector", dve_abs)
                else:
                    def act_abs(e, t=t, part=part, col=col):
                        wt(e, SGD, f"wst_{t}")
                        i = e.activation(sq2[:, :], wst[:, t % 4, :],
                                         AF.Abs,
                                         accum_out=part[:, col:col + 1])
                        inc2(i, SA, f"absred_{t}")

                    on("scalar", act_abs)

            if hasA:
                def pe_xnT(e, w=w):
                    wt(e, SV, f"ln_{w}")
                    if w == 0:
                        wt(e, SC, "setup_sync")
                    elif w == 1:
                        wt(e, SV, "bqTev_0")
                    else:
                        wt(e, SA, f"aoTev_{w-2}")
                    for c in range(8):
                        i = e.transpose(pE[:, c * WS:(c + 1) * WS],
                                        xn_t[:, w % 2, c * WS:(c + 1) * WS],
                                        ident[:, :])
                    inc2(i, SP, f"pe_xnT_{w}")

                on("tensor", pe_xnT)

                def dve_xnT(e, w=w):
                    wt(e, SP, f"pe_xnT_{w}")
                    i = e.tensor_copy(xnT[:, :, w * WS:(w + 1) * WS],
                                      pE[:, :].rearrange(
                                          "p (c n) -> p c n", n=WS))
                    inc2(i, SV, f"xnTev_{w}")

                on("vector", dve_xnT)

                def pe_k(e, w=w):
                    wt(e, SV, f"xnTev_{w}")
                    if w <= 1:
                        wt(e, SGD, "setup_k")
                    if w == 1:
                        wt(e, SV, "khat_0")
                    elif w >= 2:
                        wt(e, SV, f"ao_{w-2}")
                    for k in range(8):
                        for hf in range(2):
                            i = e.matmul(
                                pB[:, hf * 512:(hf + 1) * 512],
                                xnT[:, k, w * WS:(w + 1) * WS],
                                wqkv[:, k, D + hf * 512:D + (hf + 1) * 512],
                                start=(k == 0), stop=(k == 7))
                    inc2(i, SP, f"pe_k_{w}")

                on("tensor", pe_k)

                def act_sqk(e, w=w):
                    wt(e, SP, f"pe_k_{w}")
                    if w >= 1:
                        wt(e, SV, f"redk_{w-1}")
                    i = e.activation(sq2[:, :], pB[:, 0:D], AF.Square)
                    inc2(i, SA, f"sqk_{w}")

                on("scalar", act_sqk)

                def dve_redk(e, w=w):
                    wt(e, SA, f"sqk_{w}")
                    i = e.tensor_reduce(
                        n2[:, :], sq2[:, :].rearrange("p (h d) -> p h d", d=DH),
                        AX, OP.add)
                    inc(e, SV, f"redk_{w}")

                on("vector", dve_redk)

                def act_sqrtk(e, w=w):
                    wt(e, SV, f"redk_{w}")
                    e.activation(n2[:, :], n2[:, :], AF.Ln,
                                 bias=cvals[:, 1:2])
                    gap(e)
                    i = e.activation(rn[:, :], n2[:, :], AF.Exp, scale=-0.5)
                    inc(e, SA, f"sqrtk_{w}")

                on("scalar", act_sqrtk)

                def dve_khat(e, w=w):
                    wt(e, SA, f"sqrtk_{w}")
                    i = e.tensor_tensor(
                        k_hat[:, :].rearrange("p (h d) -> p h d", d=DH),
                        pB[:, 0:D].rearrange("p (h d) -> p h d", d=DH),
                        rn[:, :].unsqueeze(2).broadcast_to((128, H, DH)),
                        OP.mult)
                    inc2(i, SV, f"khat_{w}")

                on("vector", dve_khat)

                def dve_rotk(e, w=w):
                    kh3 = k_hat[:, :].rearrange("p (h d) -> p h d", d=DH)
                    rt3 = rtmp[:, :].rearrange("p (h d) -> p h d", d=DH)
                    e.tensor_tensor(
                        rt3[:, :, 0:32], kh3[:, :, 32:64],
                        Sk[:, 0:32].unsqueeze(1).broadcast_to((WS, H, 32)),
                        OP.mult)
                    e.tensor_tensor(
                        rt3[:, :, 32:64], kh3[:, :, 0:32],
                        Sk[:, 32:64].unsqueeze(1).broadcast_to((WS, H, 32)),
                        OP.mult)
                    e.tensor_tensor(
                        kh3, kh3,
                        Ck[:, :].unsqueeze(1).broadcast_to((WS, H, DH)),
                        OP.mult)
                    i = e.tensor_tensor(krot1[:, :], k_hat[:, :], rtmp[:, :],
                                        OP.add)
                    inc2(i, SV, f"krot_{w}")

                on("vector", dve_rotk)

                def pe_v(e, w=w):
                    wt(e, SV, f"xnTev_{w}")
                    if w <= 1:
                        wt(e, SGD, "setup_v")
                    if w >= 2:
                        wt(e, SA, f"sev_{w-2}")
                    elif w == 1:
                        wt(e, SV, "qhat_0")
                    for k in range(8):
                        for hf in range(2):
                            i = e.matmul(
                                pA[:, hf * 512:(hf + 1) * 512],
                                xnT[:, k, w * WS:(w + 1) * WS],
                                wqkv[:, k, 2 * D + hf * 512:
                                     2 * D + (hf + 1) * 512],
                                start=(k == 0), stop=(k == 7))
                    inc2(i, SP, f"pe_v_{w}")

                on("tensor", pe_v)

                def act_v(e, w=w):
                    wt(e, SP, f"pe_v_{w}")
                    if w >= 4:
                        wt(e, SP, f"pe_av_{w-3}_15")
                    i = e.copy(v_ring[:, w % 4, :].rearrange(
                                   "p (h c) -> p h c", c=65)[:, :, 0:64],
                               pA[:, :].rearrange("p (h d) -> p h d", d=DH))
                    inc2(i, SA, f"vev_{w}")

                on("scalar", act_v)

                def pe_q(e, w=w):
                    wt(e, SV, f"xnTev_{w}")
                    if w <= 1:
                        wt(e, SGD, "setup_q")
                    wt(e, SA, f"vev_{w}")
                    for k in range(8):
                        for hf in range(2):
                            i = e.matmul(
                                pA[:, hf * 512:(hf + 1) * 512],
                                xnT[:, k, w * WS:(w + 1) * WS],
                                wqkv[:, k, hf * 512:(hf + 1) * 512],
                                start=(k == 0), stop=(k == 7))
                    inc2(i, SP, f"pe_q_{w}")

                on("tensor", pe_q)

                def act_sqq(e, w=w):
                    wt(e, SP, f"pe_q_{w}")
                    if w >= 1:
                        wt(e, SV, f"redq_{w-1}")
                    i = e.activation(sq1[:, :], pA[:, :], AF.Square)
                    inc2(i, SA, f"sqq_{w}")

                on("scalar", act_sqq)

                def dve_redq(e, w=w):
                    wt(e, SA, f"sqq_{w}")
                    i = e.tensor_reduce(
                        n2[:, :], sq1[:, :].rearrange("p (h d) -> p h d", d=DH),
                        AX, OP.add)
                    inc(e, SV, f"redq_{w}")

                on("vector", dve_redq)

                def act_sqrtq(e, w=w):
                    wt(e, SV, f"redq_{w}")
                    e.activation(n2[:, :], n2[:, :], AF.Ln,
                                 bias=cvals[:, 1:2])
                    gap(e)
                    i = e.activation(rn[:, :], n2[:, :], AF.Exp, scale=-0.5)
                    inc(e, SA, f"sqrtq_{w}")

                on("scalar", act_sqrtq)

                def dve_qhat(e, w=w):
                    wt(e, SA, f"sqrtq_{w}")
                    i = e.tensor_tensor(
                        q_hat[:, :].rearrange("p (h d) -> p h d", d=DH),
                        pA[:, :].rearrange("p (h d) -> p h d", d=DH),
                        rn[:, :].unsqueeze(2).broadcast_to((128, H, DH)),
                        OP.mult)
                    inc2(i, SV, f"qhat_{w}")

                on("vector", dve_qhat)

                if w >= 1:
                    def pe_cr0(e, w=w):
                        wt(e, SV, f"khat_{w}")
                        wt(e, SA, f"kT1_{w-1}")
                        src = bkT[:, (w - 1) % 3, :, WS:2 * WS]
                        for nh in range(2):
                            e.matmul(pB[:, nh * 512:(nh + 1) * 512],
                                     R0Ts[:, 0, :],
                                     src[:, nh * 4:(nh + 1) * 4, :],
                                     start=True, stop=False)
                            i = e.matmul(pB[:, nh * 512:(nh + 1) * 512],
                                         R0Ts[:, 1, :],
                                         src[:, nh * 4:(nh + 1) * 4, :],
                                         start=False, stop=True)
                        inc2(i, SP, f"cr0mm_{w}")

                    on("tensor", pe_cr0)

                    def dve_cr0ev(e, w=w):
                        wt(e, SP, f"cr0mm_{w}")
                        if w >= 3:
                            wt(e, SP, f"pe_sim_{w-3}_15")
                        i = e.tensor_copy(
                            bkT[:, w % 3, :, 0:WS],
                            pB[:, 0:D].rearrange("p (a b) -> p a b", b=WS))
                        inc2(i, SV, f"cr0_{w}")

                    on("vector", dve_cr0ev)

                def pe_kT1(e, w=w):
                    wt(e, SV, f"krot_{w}")
                    wt(e, SV, f"xnTev_{w}")
                    for c in range(8):
                        i = e.transpose(pE[:, c * WS:(c + 1) * WS],
                                        krot1[:, c * WS:(c + 1) * WS],
                                        ident[:, :])
                    inc2(i, SP, f"pe_kT1_{w}")

                on("tensor", pe_kT1)

                def act_kT1(e, w=w):
                    wt(e, SP, f"pe_kT1_{w}")
                    if w >= 3:
                        wt(e, SP, f"pe_sim_{w-3}_15")
                    i = e.copy(bkT[:, w % 3, :, WS:2 * WS],
                               pE[:, :].rearrange("p (c n) -> p c n", n=WS))
                    inc2(i, SA, f"kT1_{w}")

                on("scalar", act_kT1)

                if w >= 1:
                    def pe_cr2(e, w=w):
                        wt(e, SA, f"kT1_{w}")
                        wt(e, SV, f"cr0_{w}")
                        src = bkT[:, w % 3, :, WS:2 * WS]
                        for nh in range(2):
                            e.matmul(pB[:, nh * 512:(nh + 1) * 512],
                                     R2Ts[:, 0, :],
                                     src[:, nh * 4:(nh + 1) * 4, :],
                                     start=True, stop=False)
                            i = e.matmul(pB[:, nh * 512:(nh + 1) * 512],
                                         R2Ts[:, 1, :],
                                         src[:, nh * 4:(nh + 1) * 4, :],
                                         start=False, stop=True)
                        inc2(i, SP, f"cr2mm_{w-1}")

                    on("tensor", pe_cr2)

                    def dve_cr2ev(e, w=w):
                        wt(e, SP, f"cr2mm_{w-1}")
                        if w >= 4:
                            wt(e, SP, f"pe_sim_{w-4}_15")
                        i = e.tensor_copy(
                            bkT[:, (w - 1) % 3, :, 2 * WS:3 * WS],
                            pB[:, 0:D].rearrange("p (a b) -> p a b", b=WS))
                        inc2(i, SV, f"cr2_{w-1}")

                    on("vector", dve_cr2ev)

                def dve_rotq(e, w=w):
                    if w >= 3:
                        wt(e, SP, f"pe_bqT_{w-3}")
                    qh3 = q_hat[:, :].rearrange("p (h d) -> p h d", d=DH)
                    e.tensor_tensor(
                        rtmp[:, :].rearrange("p (h d) -> p h d", d=DH)[:, :, 0:32],
                        qh3[:, :, 32:64],
                        Sq[:, 0:32].unsqueeze(1).broadcast_to((WS, H, 32)),
                        OP.mult)
                    e.tensor_tensor(
                        rtmp[:, :].rearrange("p (h d) -> p h d", d=DH)[:, :, 32:64],
                        qh3[:, :, 0:32],
                        Sq[:, 32:64].unsqueeze(1).broadcast_to((WS, H, 32)),
                        OP.mult)
                    e.tensor_tensor(
                        q_hat[:, :].rearrange("p (h d) -> p h d", d=DH),
                        qh3,
                        Cq[:, :].unsqueeze(1).broadcast_to((WS, H, DH)),
                        OP.mult)
                    i = e.tensor_tensor(bq[:, w % 3, :], q_hat[:, :],
                                        rtmp[:, :], OP.add)
                    inc2(i, SV, f"bq_{w}")

                on("vector", dve_rotq)

            for t in range(4 * w, min(4 * w + 4, len(WTILES))):
                emit_wtile(t)

        if TRUNC is None:
            EM["p1_pe"] = SP.c
            EM["p1_dve"] = SV.c
            EM["p1_act"] = SA.c
            EM["p1_pool"] = SG.c

            def p2_join(e):
                wt(e, SP, "p1_pe")
                wt(e, SV, "p1_dve")
                wt(e, SA, "p1_act")
                wt(e, SG, "p1_pool")

            p1.close()
            p2 = ExitStack()

            def sb2(name, shape, dt=F32):
                return sb(name, shape, dt, pool=p2)

            acc = sb2("acc", [128, 16, D])
            W1c = sb2("W1c", [128, 2, 8, 1024], BF16)
            W2c = sb2("W2c", [128, 8, 1024], BF16)
            hT = sb2("hT", [128, 8, 1024], BF16)
            wst2 = sb2("wst2", [128, 4, 1024])
            b1t_s = sb2("b1t_s", [128, 32])
            b2b_s = sb2("b2b_s", [128, D])
            g_t = sb2("g_t", [128, D])
            s2 = sb2("s2", [128, 2, D], BF16)
            x2 = sb2("x2", [128, 2, D])
            o2 = sb2("o2", [128, 2, D])
            tmagA = g_t[:, :]
            tmagB = o2[:, 0, :]

            f1 = [p2.enter_context(nc.psum_tensor(f"f1_{i}", [128, 512], F32))
                  for i in range(2)]
            f2 = [p2.enter_context(nc.psum_tensor(f"f2_{i}", [128, D], F32))
                  for i in range(3)]

            def sync_p2a(e):
                p2_join(e)
                sync_dma(e, b1t_s[:, :], b1t_d, SD)
                sync_dma(e, b2b_s[:, :], b2b_d, SD, "p2consts")

            on("sync", sync_p2a)

            def emit_scale(widx, part, outcol):
                def dve_a(e, part=part, widx=widx):
                    if widx == 1:
                        p2_join(e)
                    i = e.tensor_reduce(smallB[:, 0:1], part[:, :], AX, OP.add)
                    inc(e, SV, f"scred{widx}")

                on("vector", dve_a)

                def pe_a(e, widx=widx):
                    wt(e, SV, f"scred{widx}")
                    if widx == 1:
                        p2_join(e)
                    i = e.matmul(f1[0][0:1, 0:1], smallB[:, 0:1], onesc[:, :],
                                 start=True, stop=True)
                    inc(e, SP, f"scmm{widx}")

                on("tensor", pe_a)

                def dve_b(e, widx=widx):
                    wt(e, SP, f"scmm{widx}")
                    e.tensor_scalar(smallA[0:1, 2:3], f1[0][0:1, 0:1],
                                    1.0 / (D * FF), 1e-8, OP.mult, OP.max)
                    gap(e)
                    i = e.reciprocal(smallA[0:1, 3:4], smallA[0:1, 2:3])
                    inc(e, SV, f"screc{widx}")

                on("vector", dve_b)

                def pe_b(e, widx=widx):
                    wt(e, SV, f"screc{widx}")
                    i = e.matmul(f1[0][:, 1:2], onesr[:, :], smallA[0:1, 3:4],
                                 start=True, stop=True)
                    inc(e, SP, f"scbc{widx}")

                on("tensor", pe_b)

                def dve_c(e, widx=widx, outcol=outcol):
                    wt(e, SP, f"scbc{widx}")
                    i = e.tensor_copy(outcol[:, :], f1[0][:, 1:2])
                    inc(e, SV, f"rs{widx}")

                on("vector", dve_c)

            emit_scale(1, partials1, rs1col)
            emit_scale(2, partials2, rs2col)

            def emit_combine(tn):
                def gp_xs(e, tn=tn):
                    wt(e, SD, f"sst_{tn}")
                    if tn >= 2:
                        wt(e, SV, f"comb_{tn-2}")
                    pool_dma(e, x2[:, tn % 2, :], x_t4[tn], f"x2_{tn}")
                    pool_dma(e, s2[:, tn % 2, :], s_t4[tn], f"s2_{tn}")

                on("gpsimd", gp_xs)

                def act_silu(e, tn=tn):
                    wt(e, SV, f"accadd_3_{tn // 8}_{tn % 8}")
                    if tn >= 1:
                        wt(e, SV, f"comb_{tn-1}")
                    i = e.activation(g_t[:, :], acc[:, tn, :], AF.Silu)
                    inc2(i, SA, f"silu_{tn}")

                on("scalar", act_silu)

                def dve_comb(e, tn=tn):
                    wt(e, SA, f"silu_{tn}")
                    wt(e, SGD, f"s2_{tn}")
                    wt(e, SGD, f"x2_{tn}")
                    if tn >= 2:
                        wt(e, SO, f"ost_{tn-2}")
                    e.tensor_tensor(g_t[:, :], s2[:, tn % 2, :], g_t[:, :],
                                    OP.mult)
                    i = e.tensor_tensor(o2[:, tn % 2, :], x2[:, tn % 2, :],
                                        g_t[:, :], OP.add)
                    inc2(i, SV, f"comb_{tn}")

                on("vector", dve_comb)

                def sync_out(e, tn=tn):
                    wt(e, SV, f"comb_{tn}")
                    sync_dma(e, out_t4[tn], o2[:, tn % 2, :], SO, f"ost_{tn}")

                on("sync", sync_out)

            def wtile_src(c, which, k):
                return w1t_t[k, c] if which == 0 else w2t_t[c, k]

            def wtile_of(t):
                c, r = divmod(t, 16)
                which, k = divmod(r, 8)
                return c, which, k

            W1LIST = [c_ * 16 + k_ for c_ in range(4) for k_ in range(8)]
            W2LIST = [c_ * 16 + 8 + k_ for c_ in range(4) for k_ in range(8)]
            W1IDX = {t_: i_ for i_, t_ in enumerate(W1LIST)}
            W2IDX = {t_: i_ for i_, t_ in enumerate(W2LIST)}
            MAGIC = 12582912.0

            def emit_tern_core(t, slot, ready):
                c, r = divmod(t, 16)
                which, k = divmod(r, 8)
                dst = (W1c[:, c % 2] if which == 0 else W2c)
                rcol = rs1col if which == 0 else rs2col

                def dve_tern(e, t=t, slot=slot, dst=dst, rcol=rcol,
                             k=k, c=c, which=which, ready=ready):
                    wt(e, ready[0], ready[1])
                    if c == 0:
                        wt(e, SV, f"rs{1 + which}")
                    if which == 0:
                        if c >= 2:
                            wt(e, SP, f"f1_{c-2}_1_15")
                    else:
                        if c >= 1:
                            wt(e, SP, f"ffn2_{c-1}_1_7")
                    e.tensor_scalar(tmagA, wst2[:, slot, :],
                                    rcol[:, :], MAGIC, OP.mult, OP.add)
                    e.tensor_scalar(tmagB, tmagA, MAGIC, 1.0,
                                    OP.subtract, OP.min)
                    i = e.tensor_scalar(dst[:, k, :], tmagB,
                                        -1.0, None, OP.max)
                    inc2(i, SV, f"tern_{t}")

                on("vector", dve_tern)

            def emit_tern_tile(t):
                c, r = divmod(t, 16)
                which, k = divmod(r, 8)
                if which == 0:
                    i_ = W1IDX[t]

                    def gp_dma(e, t=t, i_=i_):
                        if i_ == 0:
                            p2_join(e)
                            pool_dma(e, wst2[:, 0, :],
                                     wtile_src(0, 0, 0), "w1st_0")
                        wt(e, SGD, f"w1st_{t}")
                        if i_ >= 1:
                            wt(e, SV, f"tern_{W1LIST[i_-1]}")
                        if i_ + 1 < 32:
                            nt = W1LIST[i_ + 1]
                            nc_, nw_, nk_ = wtile_of(nt)
                            pool_dma(e, wst2[:, (i_ + 1) % 2, :],
                                     wtile_src(nc_, nw_, nk_),
                                     f"w1st_{nt}")
                        if PASS["n"] == 2:
                            e.engine_nop().then_inc(SG.h, 1)
                        SG.c += 1
                        if PASS["n"] == 1:
                            EM[f"wrdy_{t}"] = SG.c
                        else:
                            assert EM[f"wrdy_{t}"] == SG.c

                    on("gpsimd", gp_dma)
                    emit_tern_core(t, W1IDX[t] % 2, (SG, f"wrdy_{t}"))
                else:
                    i_ = W2IDX[t]

                    def sync_w2(e, t=t, i_=i_, c=c, k=k):
                        if i_ >= 2:
                            wt(e, SV, f"tern_{W2LIST[i_-2]}")
                        sync_dma(e, wst2[:, 2 + i_ % 2, :],
                                 wtile_src(c, 1, k), SD, f"w2st_{t}")

                    on("sync", sync_w2)
                    emit_tern_core(t, 2 + W2IDX[t] % 2, (SD, f"w2st_{t}"))

            for t in range(16):
                emit_tern_tile(t)

            for c in range(4):
                cb = c % 2
                if c >= 1:
                    for t in range(c * 16 + 8, c * 16 + 16):
                        emit_tern_tile(t)
                for half in range(2):
                    for ffs in range(8):
                        for tq in range(2):
                            pi = (ffs * 2 + tq) % 2

                            def pe_f1(e, c=c, cb=cb, half=half, ffs=ffs, tq=tq,
                                      pi=pi):
                                if ffs == 0 and tq == 0:
                                    wt(e, SV, f"tern_{c * 16 + 7}")
                                    wt(e, SD, "p2consts")
                                idx = ffs * 2 + tq
                                if idx >= 2:
                                    wt(e, SA, f"gelu_{c}_{half}_{idx-2}")
                                tok0 = half * 1024 + tq * 512
                                for k in range(8):
                                    i = e.matmul(
                                        f1[pi][:, :],
                                        W1c[:, c % 2, k,
                                            ffs * 128:(ffs + 1) * 128],
                                        xnT[:, k, tok0:tok0 + 512],
                                        start=(k == 0), stop=(k == 7))
                                inc2(i, SP, f"f1_{c}_{half}_{idx}")

                            on("tensor", pe_f1)

                            def act_gelu(e, c=c, half=half, ffs=ffs, tq=tq,
                                         pi=pi):
                                idx = ffs * 2 + tq
                                wt(e, SP, f"f1_{c}_{half}_{idx}")
                                if c == 0 and half == 0 and idx == 0:
                                    wt(e, SD, "p2consts")
                                i = e.activation(
                                    hT[:, ffs, tq * 512:(tq + 1) * 512],
                                    f1[pi][:, :], AF.Gelu,
                                    bias=b1t_s[:, c * 8 + ffs:c * 8 + ffs + 1],
                                    scale=1.0)
                                inc2(i, SA, f"gelu_{c}_{half}_{idx}")

                            on("scalar", act_gelu)

                    if half == 1 and c + 1 < 4:
                        for t in range((c + 1) * 16, (c + 1) * 16 + 8):
                            emit_tern_tile(t)

                    for tt in range(8):
                        pi = tt % 3

                        def pe_f2(e, c=c, cb=cb, half=half, tt=tt, pi=pi):
                            if tt == 0:
                                wt(e, SA, f"gelu_{c}_{half}_15")
                                wt(e, SV, f"tern_{c * 16 + 15}")
                            if tt >= 3:
                                wt(e, SV, f"accadd_{c}_{half}_{tt-3}")
                            else:
                                pv = {0: 6, 1: 7, 2: 5}[tt]
                                if half == 1:
                                    wt(e, SV, f"accadd_{c}_0_{pv}")
                                elif c >= 1:
                                    wt(e, SV, f"accadd_{c-1}_1_{pv}")
                            for k in range(8):
                                for nh in range(2):
                                    i = e.matmul(
                                        f2[pi][:, nh * 512:(nh + 1) * 512],
                                        hT[:, k, tt * 128:(tt + 1) * 128],
                                        W2c[:, k, nh * 512:(nh + 1) * 512],
                                        start=(k == 0), stop=(k == 7))
                            inc2(i, SP, f"ffn2_{c}_{half}_{tt}")

                        on("tensor", pe_f2)

                        def dve_acc(e, c=c, half=half, tt=tt, pi=pi):
                            wt(e, SP, f"ffn2_{c}_{half}_{tt}")
                            if c == 0 and half == 0 and tt == 0:
                                wt(e, SD, "p2consts")
                            tn = half * 8 + tt
                            if c == 0:
                                i = e.tensor_tensor(acc[:, tn, :], f2[pi][:, :],
                                                    b2b_s[:, :], OP.add)
                            else:
                                i = e.tensor_tensor(acc[:, tn, :], acc[:, tn, :],
                                                    f2[pi][:, :], OP.add)
                            inc2(i, SV, f"accadd_{c}_{half}_{tt}")

                        on("vector", dve_acc)

                        if c == 3:
                            emit_combine(half * 8 + tt)

            def emit_taps():
                specs = {
                    "s_scr": (lambda: s_scr, ("SD", "sst_15")),
                    "xnT": (lambda: P["xnT"][:, :, :].rearrange(
                        "p a b -> p (a b)"), ("SA", "p1_act")),
                    "acc": (lambda: P["acc"][:, :, :].rearrange(
                        "p a b -> p (a b)"), ("SV", "accadd_3_1_7")),
                    "bkT": (lambda: P["bkT"][:, :, :, :].rearrange(
                        "p a b c -> p (a b c)"), ("SV", "accadd_3_1_7")),
                    "bq": (lambda: P["bq"][:, :, :].rearrange(
                        "p a b -> p (a b)"), ("SV", "accadd_3_1_7")),
                    "den": (lambda: P["den"][:, :, :].rearrange(
                        "p a b -> p (a b)"), ("SV", "accadd_3_1_7")),
                    "negb": (lambda: P["negb"][:, :, :].rearrange(
                        "p a b -> p (a b)"), ("SV", "accadd_3_1_7")),
                    "aoALL": (lambda: P["aoALL"][:, :], ("SV", "accadd_3_1_7")),
                    "krot1": (lambda: P["krot1"][:, :], ("SV", "accadd_3_1_7")),
                    "partials1": (lambda: P["partials1"][:, :],
                                  ("SV", "accadd_3_1_7")),
                    "partials2": (lambda: P["partials2"][:, :],
                                  ("SV", "accadd_3_1_7")),
                    "rs1col": (lambda: P["rs1col"][:, :], ("SV", "accadd_3_1_7")),
                    "rs2col": (lambda: P["rs2col"][:, :], ("SV", "accadd_3_1_7")),
                    "W1c": (lambda: P["W1c"][:, :, :].rearrange(
                        "p a b -> p (a b)"), ("SV", "accadd_3_1_7")),
                    "W2c": (lambda: P["W2c"][:, :, :].rearrange(
                        "p a b -> p (a b)"), ("SV", "accadd_3_1_7")),
                    "hT": (lambda: P["hT"][:, :, :].rearrange(
                        "p a b -> p (a b)"), ("SV", "accadd_3_1_7")),
                    "b1t_s": (lambda: P["b1t_s"][:, :], ("SV", "accadd_3_1_7")),
                    "den0": (lambda: P["den"][:, 0, :], ("SV", "ao_0")),
                    "negb0": (lambda: P["negb"][:, 0, :], ("SV", "negb_0")),
                    "p00": (lambda: P["p_sb"][:, 0, :], ("SA", "exp_0_0")),
                    "pT00": (lambda: P["pT_sb"][:, 0, :], ("SV", "pTev_0_0")),
                    "aoALL0": (lambda: P["aoALL"][:, :], ("SV", "ao_0")),
                    "bq0": (lambda: P["bq"][:, 0, :], ("SV", "bq_0")),
                    "bkT0": (lambda: P["bkT"][:, 0, :, :].rearrange(
                        "p a b -> p (a b)"), ("SP", "pe_sim_0_15")),
                    "stats0": (lambda: P["stats"][:, :], ("SV", "ln_0")),
                }
                sems = {"SD": SD, "SA": SA, "SV": SV, "SP": SP}
                for nm in tap_d:
                    fn, (semn, tag) = specs[nm]

                    def body(e, nm=nm, fn=fn, semn=semn, tag=tag):
                        wt(e, sems[semn], tag)
                        sync_dma(e, tap_d[nm], fn(), SO)

                    on("sync", body)

            emit_taps()
            EM["final_so"] = SO.c

            def sync_fin(e):
                e.wait_ge(SO.h, EM["final_so"])

            on("sync", sync_fin)

        else:
            def dbg_fin(e):
                if TRUNC >= 3:
                    wt(e, SD, f"sst_{TRUNC-3}")
                else:
                    wt(e, SC, "setup_sync")

            on("sync", dbg_fin)
            p1.close()

        PASS["n"] = 2
        for sem in (SD, SC, SGD, SP, SV, SA, SG, SO):
            sem.c = 0
        SYNC_RING["sd"] = 0
        SYNC_RING["so"] = 0
        for key in list(EM):
            pass
        bodies2 = list(BODIES)
        BODIES.clear()

        with nc.Block() as blk:
            def run_eng(engname):
                def closure(e):
                    for eng, body in bodies2:
                        if eng == engname:
                            body(e)
                return closure
            blk.sync(run_eng("sync"))
            blk.tensor(run_eng("tensor"))
            blk.vector(run_eng("vector"))
            blk.scalar(run_eng("scalar"))
            blk.gpsimd(run_eng("gpsimd"))

        if TRUNC is None:
            p2.close()

    return nc


_NC_CACHE = {}


def get_nc():
    if "nc" not in _NC_CACHE:
        _NC_CACHE["nc"] = build_nc()
    return _NC_CACHE["nc"]


def kernel(L0_attn_x, L0_attn_w_qkv, L0_attn_w_out, L0_attn_q_scale,
           L0_attn_k_scale, L0_ffn_W1, L0_ffn_b1, L0_ffn_W2, L0_ffn_b2):
    consts = host_consts()
    shared = host_inputs({
        "L0_attn_w_qkv": L0_attn_w_qkv, "L0_attn_w_out": L0_attn_w_out,
        "L0_attn_q_scale": L0_attn_q_scale, "L0_attn_k_scale": L0_attn_k_scale,
        "L0_ffn_W1": L0_ffn_W1, "L0_ffn_b1": L0_ffn_b1,
        "L0_ffn_W2": L0_ffn_W2, "L0_ffn_b2": L0_ffn_b2})
    x = np.ascontiguousarray(np.asarray(L0_attn_x, np.float32))
    base = {}
    base.update(consts)
    base.update(shared)
    in_maps = []
    for c in range(NCORE):
        m = dict(base)
        m["x"] = np.ascontiguousarray(x[c])
        in_maps.append(m)
    nc = get_nc()
    res = run_bass_kernel_spmd(nc, in_maps, core_ids=list(range(NCORE)))
    out = np.stack([res.results[c]["out"] for c in range(NCORE)], axis=0)
    return out
